# version 04
# baseline (speedup 1.0000x reference)
import numpy as np
import concourse.bass as bass
import concourse.mybir as mybir
from concourse.bass_utils import run_bass_kernel_spmd

F32 = mybir.dt.float32
BF16 = mybir.dt.bfloat16
AF = mybir.ActivationFunctionType
OP = mybir.AluOpType
AX = mybir.AxisListType

D = 1024
DI = 2048
T = 2048
DS = 16
HC = 8
IC = 16
TCM = 256
NCH = T // TCM
SUP = 1024
FT = 512
NST = T // SUP
NTS = SUP // FT
EPS = 1e-5

ENGINES = ("tensor", "vector", "scalar", "gpsimd", "sync")
QUEUES = ("qsync", "qgps")


class Plan:

    def __init__(self):
        self.ops = []
        self.fences = []

    def op(self, stream, fn, reads=(), writes=(), queue=None, tag=""):
        assert stream in ENGINES
        self.ops.append(dict(
            stream=stream, fn=fn, reads=tuple(reads), writes=tuple(writes),
            queue=queue, tag=tag, deps=[], signal=False, sig=None, waits=None,
        ))

    def dma(self, stream, fn, reads=(), writes=(), tag=""):
        q = "qgps" if stream == "gpsimd" else "qsync"
        self.op(stream, fn, reads, writes, queue=q, tag=tag)

    def fence(self):
        self.fences.append(len(self.ops))

    PARANOID = False

    def analyze(self):
        last_w = {}
        readers = {}
        fence_set = set(self.fences)
        last_of = {}
        fence_deps = None
        fence_waiting = set()
        for i, o in enumerate(self.ops):
            if i in fence_set:
                fence_deps = dict(last_of)
                fence_waiting = set(ENGINES)
            deps = set()
            if fence_deps is not None and o["stream"] in fence_waiting:
                for s, j in fence_deps.items():
                    deps.add(j)
                fence_waiting.discard(o["stream"])
                if not fence_waiting:
                    fence_deps = None
            for r in o["reads"]:
                if r in last_w:
                    deps.add(last_w[r])
            for r in o["writes"]:
                if r in last_w:
                    deps.add(last_w[r])
                for j in readers.get(r, ()):
                    deps.add(j)
            own = o["queue"] or o["stream"]
            deps.discard(i)
            dd = []
            for j in deps:
                p = self.ops[j]
                psrc = p["queue"] or p["stream"]
                if psrc == own and p["queue"] is None and o["queue"] is None:
                    continue
                if p["queue"] is not None and p["queue"] == o.get("queue") \
                        and p["stream"] == o["stream"]:
                    continue
                if p["queue"] is None and p["stream"] == o["stream"]:
                    continue
                dd.append(j)
            o["deps"] = dd
            for j in dd:
                self.ops[j]["signal"] = True
            for r in o["reads"]:
                readers.setdefault(r, []).append(i)
            for r in o["writes"]:
                last_w[r] = i
                readers[r] = []
            last_of[own] = i
        cnt = {s: 0 for s in ENGINES}
        qcnt = {q: 0 for q in QUEUES}
        for o in self.ops:
            if o["queue"] is not None:
                qcnt[o["queue"]] += 1
                o["sig"] = qcnt[o["queue"]]
                o["signal"] = True
            elif o["signal"]:
                cnt[o["stream"]] += 1
                o["sig"] = cnt[o["stream"]]
        self.final_qcnt = qcnt
        clocks = {s: {} for s in ENGINES}

        def merge(c, snap):
            for k, v in snap.items():
                if c.get(k, 0) < v:
                    c[k] = v

        for o in self.ops:
            s = o["stream"]
            c = clocks[s]
            waits = {}
            for j in o["deps"]:
                p = self.ops[j]
                src = p["queue"] or p["stream"]
                v = p["sig"]
                assert v is not None
                if not Plan.PARANOID and c.get(src, 0) >= v:
                    continue
                if waits.get(src, 0) < v:
                    waits[src] = v
            for src, v in waits.items():
                pass
            for j in o["deps"]:
                p = self.ops[j]
                src = p["queue"] or p["stream"]
                if src in waits and p["sig"] <= waits[src]:
                    merge(c, p.get("snap", {}))
            for src, v in waits.items():
                if c.get(src, 0) < v:
                    c[src] = v
            o["waits"] = sorted(waits.items())
            if o["signal"]:
                src = o["queue"] or s
                snap = dict(c)
                snap[src] = o["sig"]
                if o["queue"] is None:
                    snap[s] = o["sig"]
                o["snap"] = snap

    def emit(self, nc, sems):
        engine_ops = {s: [] for s in ENGINES}
        for o in self.ops:
            engine_ops[o["stream"]].append(o)

        def body(eng, s):
            for o in engine_ops[s]:
                for src, v in o["waits"]:
                    val = v * 16 if src in QUEUES else v
                    eng.wait_ge(sems[src], val)
                inst = o["fn"](eng)
                if o["queue"] is not None:
                    inst.then_inc(sems[o["queue"]], 16)
                elif o["signal"]:
                    inst.then_inc(sems[o["stream"]], 1)
        return body


def ceil_div(a, b):
    return -(-a // b)


kernel_scan_stub = False


def build_nc(debug=False, nch=NCH, do_ffn=True):
    nc = bass.Bass("TRN2", target_bir_lowering=False)
    dbg_outs = {}

    dr = {}
    def din(name, shape):
        dr[name] = nc.dram_tensor(name, list(shape), F32, kind="ExternalInput").ap()
    din("xT", (D, T))
    din("wiT", (D, DI))
    din("woT", (DI, D))
    din("xpT", (DI, 33))
    din("Als", (128, 256))
    din("wdt", (128, 16))
    din("bdt", (128, 16))
    din("cw", (128, 64))
    din("Dv", (128, 16))
    din("nw", (128, 8))
    din("nb", (128, 8))
    din("gw", (D, 8))
    din("ones34", (34, 128))
    din("selmat", (33, 33 * 128))
    din("ident", (128, 128))
    din("onescol", (128, 1))
    din("shg", (D, DI))
    din("shu", (D, DI))
    din("shd", (DI, D))
    din("rtg", (8, D, DI))
    din("rtu", (8, D, DI))
    din("rtd", (8, DI, D))
    x1h = nc.dram_tensor("x1h", [D, T], F32, kind="Internal").ap()
    rtg_b = nc.dram_tensor("rtg_b", [8, D, DI], BF16, kind="Internal").ap()
    rtu_b = nc.dram_tensor("rtu_b", [8, D, DI], BF16, kind="Internal").ap()
    rtd_b = nc.dram_tensor("rtd_b", [8, DI, D], BF16, kind="Internal").ap()
    shg_b = nc.dram_tensor("shg_b", [D, DI], BF16, kind="Internal").ap()
    shu_b = nc.dram_tensor("shu_b", [D, DI], BF16, kind="Internal").ap()
    shd_b = nc.dram_tensor("shd_b", [DI, D], BF16, kind="Internal").ap()
    outT = nc.dram_tensor("outT", [D, T], F32, kind="ExternalOutput").ap()

    import contextlib
    ctx = contextlib.ExitStack()
    e = ctx.enter_context
    t_Als = e(nc.sbuf_tensor("t_Als", [128, 256], F32))
    t_wdt = e(nc.sbuf_tensor("t_wdt", [128, 16], F32))
    t_bdt = e(nc.sbuf_tensor("t_bdt", [128, 16], F32))
    t_cw = e(nc.sbuf_tensor("t_cw", [128, 64], F32))
    t_Dv = e(nc.sbuf_tensor("t_Dv", [128, 16], F32))
    t_nw = e(nc.sbuf_tensor("t_nw", [128, 8], F32))
    t_nb = e(nc.sbuf_tensor("t_nb", [128, 8], F32))
    t_gw = e(nc.sbuf_tensor("t_gw", [128, 64], F32))
    t_ones = e(nc.sbuf_tensor("t_ones", [34, 128], F32))
    t_sel = e(nc.sbuf_tensor("t_sel", [33, 33 * 128], BF16))
    t_id = e(nc.sbuf_tensor("t_id", [128, 128], F32))
    t_oc = e(nc.sbuf_tensor("t_oc", [128, 1], F32))
    t_states = e(nc.sbuf_tensor("t_states", [128, 256], F32))
    t_srow = e(nc.sbuf_tensor("t_srow", [1, 1536], F32))
    t_sm = e(nc.sbuf_tensor("t_sm", [16, SUP], F32))
    t_smb = e(nc.sbuf_tensor("t_smb", [16, SUP], BF16))
    t_lg = e(nc.sbuf_tensor("t_lg", [16, FT], F32))
    t_rsc = e(nc.sbuf_tensor("t_rsc", [128, 96], F32))
    t_spc = e(nc.sbuf_tensor("t_spc", [1, 512], F32))
    t_projc = e(nc.sbuf_tensor("t_projc", [33, 256], F32))
    t_projb = e(nc.sbuf_tensor("t_projb", [33, 256], BF16))
    t_dt1 = e(nc.sbuf_tensor("t_dt1", [128, 256], F32))
    t_tss = e(nc.sbuf_tensor("t_tss", [128, 2048], BF16))

    BIGW = 44200
    big = e(nc.sbuf_tensor("big", [128, BIGW], F32))

    ps = [e(nc.psum_tensor(f"ps{i}", [128, 512], F32)) for i in range(8)]

    sems = {}
    for s in ENGINES + QUEUES:
        sems[s] = e(nc.semaphore(f"sem_{s}"))

    off = [0]
    def alloc(w):
        o = off[0]
        off[0] += w
        assert off[0] <= BIGW, f"big overflow {off[0]}"
        return o
    viewlog = []
    def f32v(o, w):
        viewlog.append((len(P.ops) if 'P' in dir() else -1, o, w, 'f32'))
        return big[:, o:o + w]
    def bf16v(o, w32):
        viewlog.append((len(P.ops) if 'P' in dir() else -1, o, w32, 'bf16'))
        return big[:, o:o + w32].bitcast(BF16)
    nc._viewlog = viewlog

    o_wi = alloc(8192)
    o_wo = alloc(8192)
    o_xp = alloc(272)
    o_xc = alloc(2048)
    o_lnx = alloc(2048)
    o_h0b = alloc(1024)
    o_hin = alloc(16 * 259)
    o_hb = alloc(2048)
    o_dt = alloc(4096)
    o_y = alloc(4096)
    o_yb = alloc(2048)
    o_mo = alloc(2048)
    o_sq = alloc(2048)
    o_sc = alloc(6 * 256)
    o_Bs = alloc(128)
    o_Cs = alloc(128)
    o_cacc = o_sq
    m_end = off[0]
    assert m_end <= BIGW

    off[0] = 0
    o_x1t = alloc(4096)
    o_fac = alloc(4096)
    o_fac2 = alloc(4096)
    o_ln1 = alloc(4096)
    o_scr = alloc(4096)
    o_wg = [alloc(2048), alloc(2048)]
    o_wu = [alloc(2048), alloc(2048)]
    o_wd = [alloc(2048), alloc(2048)]
    o_xs = alloc(4096)
    o_act = [alloc(1024), alloc(1024)]
    o_out = alloc(4096)
    f_end = off[0]
    assert f_end <= BIGW

    def fac_slice(h, c0, w):
        base = h * SUP + c0
        assert base // 4096 == (base + w - 1) // 4096
        o = o_fac if base < 4096 else o_fac2
        return f32v(o + base % 4096, w)

    P = Plan()

    def ld(dst_ap, src_ap, regs, cast=False, tag=""):
        s = "gpsimd" if cast else "sync"
        P.dma(s, lambda eng, d=dst_ap, sp=src_ap: eng.dma_start(out=d, in_=sp),
              reads=(), writes=regs, tag=tag)

    ld(t_Als[:], dr["Als"], ("Als",))
    ld(t_wdt[:], dr["wdt"], ("wdt",))
    ld(t_bdt[:], dr["bdt"], ("bdt",))
    ld(t_cw[:], dr["cw"], ("cw",))
    ld(t_Dv[:], dr["Dv"], ("Dv",))
    ld(t_nw[:], dr["nw"], ("nw",))
    ld(t_nb[:], dr["nb"], ("nb",))
    ld(t_gw[:].rearrange("p (c e) -> p c e", c=8), dr["gw"].rearrange("(c p) e -> p c e", p=128), ("gw",))
    ld(t_ones[:], dr["ones34"], ("ones",))
    ld(t_sel[:], dr["selmat"], ("sel",), cast=True)
    ld(t_id[:], dr["ident"], ("ident",))
    ld(t_oc[:], dr["onescol"], ("onescol",))
    ld(bf16v(o_wi, 8192).rearrange("p (c i) -> p c i", c=8), dr["wiT"].rearrange("(c p) i -> p c i", p=128),
       ("wi",), cast=True)
    ld(bf16v(o_wo, 8192).rearrange("p (c h) -> p c h", c=16), dr["woT"].rearrange("(c p) h -> p c h", p=128),
       ("wo",), cast=True)
    ld(bf16v(o_xp, 264)[:, 0:528].rearrange("p (c m) -> p c m", c=16), dr["xpT"].rearrange("(c p) m -> p c m", p=128),
       ("xp",), cast=True)

    for ee in range(8):
        ld(rtg_b[ee], dr["rtg"][ee], (f"wcg{ee}",), cast=True, tag=f"pcg{ee}")
        ld(rtu_b[ee], dr["rtu"][ee], (f"wcu{ee}",), cast=True, tag=f"pcu{ee}")
        ld(rtd_b[ee], dr["rtd"][ee], (f"wcd{ee}",), cast=True, tag=f"pcd{ee}")
    ld(shg_b, dr["shg"], ("wcg8",), cast=True, tag="pcg8")
    ld(shu_b, dr["shu"], ("wcu8",), cast=True, tag="pcu8")
    ld(shd_b, dr["shd"], ("wcd8",), cast=True, tag="pcd8")

    P.op("scalar", lambda eng: eng.activation(t_Als[:], t_Als[:], AF.Exp),
         reads=("Als",), writes=("Als",), tag="Aexp")
    P.op("vector", lambda eng: eng.tensor_scalar_mul(t_Als[:], t_Als[:], -1.0),
         reads=("Als",), writes=("Als",), tag="Aneg")
    P.op("vector", lambda eng: eng.memset(t_states[:], 0.0),
         writes=("states",), tag="st0")
    P.op("vector", lambda eng: eng.memset(t_spc[:], 0.0), writes=("spc",), tag="spc0")
    for j in range(IC):
        o = o_hin + j * 259
        P.op("vector", lambda eng, o=o: eng.memset(f32v(o, 3), 0.0),
             writes=(f"hin{j}",), tag="halo0")

    def tap(name, ap, regions, shape2):
        if not debug:
            return
        if ap.dtype != F32 and debug != "all":
            return
        o = nc.dram_tensor("dbg_" + name, list(shape2), F32, kind="ExternalOutput").ap()
        dbg_outs[name] = o
        cast = ap.dtype != F32
        s = "gpsimd" if cast else "sync"
        P.dma(s, lambda eng, a=ap, oo=o: eng.dma_start(out=oo, in_=a),
              reads=tuple(regions), writes=("dbg_" + name,), tag="tap_" + name)

    def MM(psl, lhsT, rhs, start, stop, reads, wr, tag=""):
        P.op("tensor",
             lambda eng, a=psl, b=lhsT, c=rhs, s1=start, s2=stop:
                 eng.matmul(a, b, c, start=s1, stop=s2),
             reads=reads, writes=wr, tag=tag)

    def ln_stats(src_slices, sq_dst, ps_sum, ps_sq, n_feat, tc, regs_src, inst, do_sq=True):
        nk = len(src_slices)
        for k, (apx, rg) in enumerate(src_slices):
            a2, rg2 = sq_dst(k)
            P.op("scalar", lambda eng, o=a2, i=apx: eng.activation(o, i, AF.Square),
                 reads=(rg,), writes=(rg2,), tag=f"sq{inst}")
        for k, (apx, rg) in enumerate(src_slices):
            MM(ps_sum[0][0:1, 0:tc], t_oc[:], apx, k == 0, k == nk - 1,
               (rg, "onescol"), (ps_sum[1],), tag=f"sum{inst}")
        for k in range(nk):
            a2, rg2 = sq_dst(k)
            MM(ps_sq[0][0:1, 0:tc], t_oc[:], a2, k == 0, k == nk - 1,
               (rg2, "onescol"), (ps_sq[1],), tag=f"ssq{inst}")

    rowc = [0]
    def ln_rows(ps_sum, ps_sq, n_feat, tc, inst, psr, psm):
        c0 = 0
        mu = t_srow[0:1, c0:c0 + tc]
        m2 = t_srow[0:1, 512:512 + tc]
        rs = t_srow[0:1, 1024:1024 + tc]
        rmu = f"r_mu{inst}"; rm2 = f"r_m2{inst}"; rrs = f"r_rs{inst}"
        P.op("vector", lambda eng: eng.tensor_scalar_mul(mu, ps_sum[0][0:1, 0:tc], 1.0 / n_feat),
             reads=(ps_sum[1],), writes=(rmu,), tag=f"mu{inst}")
        P.op("vector", lambda eng: eng.tensor_scalar_mul(m2, ps_sq[0][0:1, 0:tc], 1.0 / n_feat),
             reads=(ps_sq[1],), writes=(rm2,), tag=f"m2{inst}")
        P.op("vector", lambda eng: eng.tensor_tensor(rs, mu, mu, OP.mult),
             reads=(rmu,), writes=(rrs,), tag=f"musq{inst}")
        P.op("vector", lambda eng: eng.tensor_tensor(m2, m2, rs, OP.subtract),
             reads=(rm2, rrs), writes=(rm2,), tag=f"var{inst}")
        P.op("vector", lambda eng: eng.tensor_scalar_add(m2, m2, EPS),
             reads=(rm2,), writes=(rm2,), tag=f"veps{inst}")
        P.op("scalar", lambda eng: eng.activation(m2, m2, AF.Ln),
             reads=(rm2,), writes=(rm2,), tag=f"lnv{inst}")
        P.op("vector", lambda eng: eng.tensor_scalar_mul(m2, m2, -0.5),
             reads=(rm2,), writes=(rm2,), tag=f"vneg{inst}")
        P.op("scalar", lambda eng: eng.activation(rs, m2, AF.Exp),
             reads=(rm2,), writes=(rrs,), tag=f"rstd{inst}")
        P.op("vector", lambda eng: eng.tensor_tensor(mu, mu, rs, OP.mult),
             reads=(rmu, rrs), writes=(rmu,), tag=f"mup{inst}")
        MM(psr[0][:, 0:tc], t_ones[0:1, :], rs, True, True,
           (rrs, "ones"), (psr[1],), tag=f"bcr{inst}")
        MM(psm[0][:, 0:tc], t_ones[0:1, :], mu, True, True,
           (rmu, "ones"), (psm[1],), tag=f"bcm{inst}")

    for c in range(nch):
        cw0 = c * TCM
        xsrc = dr["xT"].rearrange("(h p) t -> p h t", p=128)[:, :, cw0:cw0 + TCM]
        P.dma("sync", lambda eng, sp=xsrc: eng.dma_start(out=f32v(o_xc, 2048).rearrange("p (h t) -> p h t", h=8), in_=sp),
              writes=("xc",), tag=f"ldx{c}")

        xsl = [(f32v(o_xc + h * TCM, TCM), "xc") for h in range(HC)]
        ln_stats(xsl, lambda k: (f32v(o_sq + k * TCM, TCM), "sq"),
                 ps_sum=(ps[0], "ps0"), ps_sq=(ps[1], "ps1"),
                 n_feat=D, tc=TCM, regs_src=None, inst=f"l0c{c}")
        ln_rows((ps[0], "ps0"), (ps[1], "ps1"), D, TCM, f"l0c{c}",
                psr=(ps[2], "ps2"), psm=(ps[3], "ps3"))
        for h in range(HC):
            xa = f32v(o_xc + h * TCM, TCM)
            la = f32v(o_lnx + h * TCM, TCM)
            P.op("vector", lambda eng, a=la, b=xa: eng.tensor_tensor(
                     a, b, ps[2][:, 0:TCM], OP.mult),
                 reads=("xc", "ps2"), writes=("lnx",), tag=f"l0a{c}")
            P.op("vector", lambda eng, a=la: eng.tensor_tensor(
                     a, a, ps[3][:, 0:TCM], OP.subtract),
                 reads=("lnx", "ps3"), writes=("lnx",), tag=f"l0b{c}")

        if debug and c == 0:
            tap("lnx0", f32v(o_lnx, 2048), ("lnx",), (128, 2048))
        lsl = [(f32v(o_lnx + h * TCM, TCM), "lnx") for h in range(HC)]
        ln_stats(lsl, lambda k: (f32v(o_sq + k * TCM, TCM), "sq"),
                 ps_sum=(ps[0], "ps0"), ps_sq=(ps[1], "ps1"),
                 n_feat=D, tc=TCM, regs_src=None, inst=f"lic{c}")
        ln_rows((ps[0], "ps0"), (ps[1], "ps1"), D, TCM, f"lic{c}",
                psr=(ps[2], "ps2"), psm=(ps[3], "ps3"))
        for h in range(HC):
            la = f32v(o_lnx + h * TCM, TCM)
            sqa = f32v(o_sq + h * TCM, TCM)
            ha = bf16v(o_h0b, 1024)[:, h * TCM:(h + 1) * TCM]
            P.op("vector", lambda eng, a=sqa, b=la: eng.tensor_tensor(
                     a, b, ps[2][:, 0:TCM], OP.mult),
                 reads=("lnx", "ps2"), writes=("sq",), tag=f"lia{c}")
            P.op("vector", lambda eng, a=sqa: eng.tensor_tensor(
                     a, a, ps[3][:, 0:TCM], OP.subtract),
                 reads=("sq", "ps3"), writes=("sq",), tag=f"lib{c}")
            P.op("vector", lambda eng, a=ha, b=sqa, h=h: eng.tensor_scalar(
                     a, b, t_nw[:, h:h + 1], t_nb[:, h:h + 1], OP.mult, OP.add),
                 reads=("sq", "nw", "nb"), writes=("h0b",), tag=f"lic{c}")

        if debug and c == 0:
            tap("h0b0", bf16v(o_h0b, 1024), ("h0b",), (128, 2048))
        for j in range(IC):
            bank = 4 + (j % 2)
            for k in range(HC):
                lhs = bf16v(o_wi, 8192)[:, k * 2048 + j * 128: k * 2048 + (j + 1) * 128]
                rhs = bf16v(o_h0b, 1024)[:, k * TCM:(k + 1) * TCM]
                MM(ps[bank][:, 0:TCM], lhs, rhs, k == 0, k == HC - 1,
                   ("wi", "h0b"), (f"ps{bank}",), tag=f"ip{c}_{j}")
            o = o_hin + j * 259
            if c > 0:
                P.op("vector", lambda eng, o=o: eng.tensor_copy(
                         f32v(o, 3), f32v(o + 256, 3)),
                     reads=(f"hin{j}",), writes=(f"hin{j}",), tag=f"halo{c}_{j}")
            P.op("scalar", lambda eng, o=o, b=bank: eng.copy(
                     f32v(o + 3, 256), ps[b][:, 0:TCM]),
                 reads=(f"ps{bank}",), writes=(f"hin{j}",), tag=f"ipe{c}_{j}")

        for j in range(IC):
            o = o_hin + j * 259
            acc = f32v(o_cacc + (j % 8) * TCM, TCM)
            racc = f"cacc{j % 8}"
            P.op("vector", lambda eng, a=acc, o=o, j=j: eng.tensor_scalar(
                     a, f32v(o + 0, 256), t_cw[:, j * 4:j * 4 + 1], None, OP.mult),
                 reads=(f"hin{j}", "cw"), writes=(racc,), tag=f"cv0{c}_{j}")
            for k in (1, 2, 3):
                P.op("vector", lambda eng, a=acc, o=o, j=j, k=k:
                         eng.scalar_tensor_tensor(
                             a, f32v(o + k, 256), t_cw[:, j * 4 + k:j * 4 + k + 1],
                             a, OP.mult, OP.add),
                     reads=(f"hin{j}", "cw", racc), writes=(racc,), tag=f"cv{k}{c}_{j}")
            hba = bf16v(o_hb, 2048)[:, j * TCM:(j + 1) * TCM]
            P.op("scalar", lambda eng, a=hba, b=acc: eng.activation(a, b, AF.Silu),
                 reads=(racc,), writes=(f"hb{j}",), tag=f"silu{c}_{j}")

        if debug and c == 0:
            tap("hin0", f32v(o_hin, 16 * 259), tuple(f"hin{j}" for j in range(IC)), (128, 16 * 259))
            tap("hb0", bf16v(o_hb, 2048), tuple(f"hb{j}" for j in range(IC)), (128, 4096))
        for k in range(IC):
            lhs = bf16v(o_xp, 264)[:, k * 33:(k + 1) * 33]
            rhs = bf16v(o_hb, 2048)[:, k * TCM:(k + 1) * TCM]
            MM(ps[6][0:33, 0:TCM], lhs, rhs, k == 0, k == IC - 1,
               ("xp", f"hb{k}"), ("ps6",), tag=f"xp{c}")
        P.op("scalar", lambda eng: eng.copy(t_projc[:], ps[6][0:33, 0:TCM]),
             reads=("ps6",), writes=("projc",), tag=f"xpe{c}")
        P.op("vector", lambda eng: eng.tensor_copy(t_projb[:], t_projc[:]),
             reads=("projc",), writes=("projb",), tag=f"xpb{c}")

        if debug and c == 0:
            tap("projc0", t_projc[:], ("projc",), (33, 256))
        MM(ps[7][:, 0:TCM], t_sel[0:33, 0:128], t_projb[0:33, :], True, True,
           ("projb", "sel"), ("ps7",), tag=f"dti{c}")
        for j in range(IC):
            P.op("scalar", lambda eng, j=j: eng.activation(
                     t_dt1[:], ps[7][:, 0:TCM], AF.Exp,
                     bias=t_bdt[:, j:j + 1], scale=t_wdt[:, j:j + 1]),
                 reads=("ps7", "wdt", "bdt"), writes=("dt1",), tag=f"dte{c}_{j}")
            P.op("scalar", lambda eng, j=j: eng.activation(
                     f32v(o_dt + j * TCM, TCM), t_dt1[:], AF.Ln, bias=1.0),
                 reads=("dt1",), writes=(f"dt{j}",), tag=f"dtl{c}_{j}")

        if debug and c == 0:
            tap("dt0", f32v(o_dt, 4096), tuple(f"dt{j}" for j in range(IC)), (128, 4096))
        for d in range(DS):
            bB = 4 + (d % 2) * 2
            bC = bB + 1
            MM(ps[bB][:, 0:TCM], t_sel[0:33, (1 + d) * 128:(2 + d) * 128],
               t_projb[0:33, :], True, True, ("projb", "sel"), (f"ps{bB}",),
               tag=f"Bbc{c}_{d}")
            MM(ps[bC][:, 0:TCM], t_sel[0:33, (17 + d) * 128:(18 + d) * 128],
               t_projb[0:33, :], True, True, ("projb", "sel"), (f"ps{bC}",),
               tag=f"Cbc{c}_{d}")
            P.op("scalar", lambda eng, bb=bB: eng.copy(
                     bf16v(o_Bs, 128), ps[bb][:, 0:TCM]),
                 reads=(f"ps{bB}",), writes=("Bsb",), tag=f"Bev{c}_{d}")
            P.op("scalar", lambda eng, bc=bC: eng.copy(
                     bf16v(o_Cs, 128), ps[bc][:, 0:TCM]),
                 reads=(f"ps{bC}",), writes=("Csb",), tag=f"Cev{c}_{d}")
            for j in range(IC):
                it = d * IC + j
                par = it % 2
                dAa = f32v(o_sc + par * TCM, TCM)
                bxa = f32v(o_sc + (2 + par) * TCM, TCM)
                sa = bf16v(o_sc + (4 + par) * TCM, 128)
                rta = bf16v(o_sc + par * TCM, 128)
                rdA, rbx, rs_ = (f"dA{par}", f"bx{par}", f"s{par}")
                rrt = rdA
                col = j * 16 + d
                P.op("scalar", lambda eng, a=dAa, j=j, col=col: eng.activation(
                         a, f32v(o_dt + j * TCM, TCM), AF.Exp,
                         scale=t_Als[:, col:col + 1]),
                     reads=(f"dt{j}", "Als"), writes=(rdA,), tag=f"dA{c}_{d}_{j}")
                hbj = bf16v(o_hb, 2048)[:, j * TCM:(j + 1) * TCM]
                P.op("gpsimd", lambda eng, a=bxa, b=hbj: eng.tensor_tensor(
                         a, b, bf16v(o_Bs, 128), OP.mult),
                     reads=(f"hb{j}", "Bsb"), writes=(rbx,), tag=f"bx{c}_{d}_{j}")
                if kernel_scan_stub:
                    P.op("vector", lambda eng, a=sa, d0=dAa, d1=bxa:
                             eng.tensor_tensor(a, d0, d1, OP.mult),
                         reads=(rdA, rbx, f"st{col}"), writes=(rs_,), tag=f"scan{c}_{d}_{j}")
                else:
                    P.op("vector", lambda eng, a=sa, d0=dAa, d1=bxa, col=col:
                             eng.tensor_tensor_scan(
                                 a, d0, d1, t_states[:, col:col + 1], OP.mult, OP.add),
                         reads=(rdA, rbx, f"st{col}"), writes=(rs_,), tag=f"scan{c}_{d}_{j}")
                P.op("vector", lambda eng, a=sa, col=col: eng.tensor_copy(
                         t_states[:, col:col + 1], a[:, TCM - 1:TCM]),
                     reads=(rs_,), writes=(f"st{col}",), tag=f"stc{c}_{d}_{j}")
                if debug and c == 0 and d == 0 and j == 0:
                    tap("dA00", dAa, (rdA,), (128, 256))
                    tap("bx00", bxa, (rbx,), (128, 256))
                    tap("s00", sa, (rs_,), (128, 256))
                if debug and c == 0 and d == 1 and j == 0:
                    tap("yd0", f32v(o_y, 4096), tuple(f"y{jj}" for jj in range(IC)), (128, 4096))
                    tap("s10", sa, (rs_,), (128, 256))
                    tap("dA10", dAa, (rdA,), (128, 256))
                    tap("bx10", bxa, (rbx,), (128, 256))
                if debug and c == 0 and d == 2 and j == 0:
                    tap("yd1", f32v(o_y, 256), ("y0",), (128, 256))
                if debug and c == 0 and d == 3 and j == 0:
                    tap("yd2", f32v(o_y, 256), ("y0",), (128, 256))
                if debug and c == 0 and d == 8 and j == 0:
                    tap("yd7", f32v(o_y, 256), ("y0",), (128, 256))
                ya = f32v(o_y + j * TCM, TCM)
                if d == 0:
                    P.op("vector", lambda eng, a=ya, b=sa: eng.tensor_tensor(
                             a, b, bf16v(o_Cs, 128), OP.mult),
                         reads=(rs_, "Csb"), writes=(f"y{j}",), tag=f"ro{c}_{d}_{j}")
                else:
                    P.op("vector", lambda eng, a=rta, b=sa: eng.tensor_tensor(
                             a, b, bf16v(o_Cs, 128), OP.mult),
                         reads=(rs_, "Csb"), writes=(rrt,), tag=f"ro{c}_{d}_{j}")
                    yeng = "gpsimd" if (d % 2) == 0 else "vector"
                    P.op(yeng, lambda eng, a=ya, b=rta: eng.tensor_tensor(
                             a, a, b, OP.add),
                         reads=(rrt, f"y{j}"), writes=(f"y{j}",), tag=f"ya{c}_{d}_{j}")

        if debug and c == 0:
            tap("y0", f32v(o_y, 4096), tuple(f"y{j}" for j in range(IC)), (128, 4096))
        for j in range(IC):
            ya = f32v(o_y + j * TCM, TCM)
            hbj = bf16v(o_hb, 2048)[:, j * TCM:(j + 1) * TCM]
            yba = bf16v(o_yb, 2048)[:, j * TCM:(j + 1) * TCM]
            P.op("vector", lambda eng, a=ya, b=hbj, j=j: eng.scalar_tensor_tensor(
                     a, b, t_Dv[:, j:j + 1], a, OP.mult, OP.add),
                 reads=(f"hb{j}", "Dv", f"y{j}"), writes=(f"y{j}",), tag=f"yD{c}_{j}")
            P.op("scalar", lambda eng, b=ya: eng.activation(t_dt1[:], b, AF.Silu),
                 reads=(f"y{j}",), writes=("dt1",), tag=f"ysl{c}_{j}")
            P.op("vector", lambda eng, a=yba: eng.tensor_copy(a, t_dt1[:]),
                 reads=("dt1",), writes=(f"yb{j}",), tag=f"ysc{c}_{j}")

        if debug and c == 0:
            tap("yfin", f32v(o_y, 4096), tuple(f"y{j}" for j in range(IC)), (128, 4096))
            tap("yb0", bf16v(o_yb, 2048), tuple(f"yb{j}" for j in range(IC)), (128, 4096))
        for h in range(HC):
            bank = 4 + (h % 2)
            for k in range(IC):
                lhs = bf16v(o_wo, 8192)[:, k * 1024 + h * 128: k * 1024 + (h + 1) * 128]
                rhs = bf16v(o_yb, 2048)[:, k * TCM:(k + 1) * TCM]
                MM(ps[bank][:, 0:TCM], lhs, rhs, k == 0, k == IC - 1,
                   ("wo", f"yb{k}"), (f"ps{bank}",), tag=f"op{c}_{h}")
            moa = f32v(o_mo + h * TCM, TCM)
            lna = f32v(o_lnx + h * TCM, TCM)
            P.op("vector", lambda eng, a=moa, b=lna, bk=bank: eng.tensor_tensor(
                     a, ps[bk][:, 0:TCM], b, OP.add),
                 reads=(f"ps{bank}", "lnx"), writes=("mo",), tag=f"ope{c}_{h}")

        if debug and c == 0:
            tap("mo0", f32v(o_mo, 2048), ("mo",), (128, 2048))
        msl = [(f32v(o_mo + h * TCM, TCM), "mo") for h in range(HC)]
        ln_stats(msl, lambda k: (f32v(o_sq + k * TCM, TCM), "sq"),
                 ps_sum=(ps[0], "ps0"), ps_sq=(ps[1], "ps1"),
                 n_feat=D, tc=TCM, regs_src=None, inst=f"lpc{c}")
        ln_rows((ps[0], "ps0"), (ps[1], "ps1"), D, TCM, f"lpc{c}",
                psr=(ps[2], "ps2"), psm=(ps[3], "ps3"))
        for h in range(HC):
            moa = f32v(o_mo + h * TCM, TCM)
            xa = f32v(o_xc + h * TCM, TCM)
            x1a = f32v(o_lnx + h * TCM, TCM)
            P.op("vector", lambda eng, a=x1a, b=moa: eng.tensor_tensor(
                     a, b, ps[2][:, 0:TCM], OP.mult),
                 reads=("mo", "ps2"), writes=("lnx",), tag=f"lpa{c}")
            P.op("vector", lambda eng, a=x1a: eng.tensor_tensor(
                     a, a, ps[3][:, 0:TCM], OP.subtract),
                 reads=("lnx", "ps3"), writes=("lnx",), tag=f"lpb{c}")
            P.op("vector", lambda eng, a=x1a, b=xa: eng.tensor_tensor(
                     a, a, b, OP.add),
                 reads=("lnx", "xc"), writes=("lnx",), tag=f"lpr{c}")
        if debug and c == 0:
            tap("x1c0", f32v(o_lnx, 2048), ("lnx",), (128, 2048))
        xdst = x1h.rearrange("(h p) t -> p h t", p=128)[:, :, cw0:cw0 + TCM]
        P.dma("sync", lambda eng, dp=xdst: eng.dma_start(out=dp, in_=f32v(o_lnx, 2048).rearrange("p (h t) -> p h t", h=8)),
              reads=("lnx",), writes=(f"x1h{c}",), tag=f"stx1{c}")

    P.fence()

    for st in range(NST if do_ffn else 0):
        s0 = st * SUP
        P.op("vector", lambda eng: eng.memset(f32v(o_fac, 4096), 0.0),
             writes=tuple(f"fac{h}" for h in range(4)), tag="fz")
        P.op("vector", lambda eng: eng.memset(f32v(o_fac2, 4096), 0.0),
             writes=tuple(f"fac{h}" for h in range(4, 8)), tag="fz2")

        for ts in range(NTS):
            tc0 = ts * FT
            x1src = x1h.rearrange("(h p) t -> p h t", p=128)[:, :, s0 + tc0:s0 + tc0 + FT]
            P.dma("sync", lambda eng, sp=x1src: eng.dma_start(
                      out=f32v(o_x1t, 4096).rearrange("p (h t) -> p h t", h=8), in_=sp),
                  reads=tuple(f"x1h{c}" for c in range(NCH)),
                  writes=("x1t",), tag=f"ldx1_{st}_{ts}")
            xsl = [(f32v(o_x1t + h * FT, FT), "x1t") for h in range(HC)]
            ln_stats(xsl, lambda k: (f32v(o_scr + k * FT, FT), "scr"),
                     ps_sum=(ps[0], "ps0"), ps_sq=(ps[1], "ps1"),
                     n_feat=D, tc=FT, regs_src=None, inst=f"l1_{st}_{ts}")
            ln_rows((ps[0], "ps0"), (ps[1], "ps1"), D, FT, f"l1_{st}_{ts}",
                    psr=(ps[2], "ps2"), psm=(ps[3], "ps3"))
            for h in range(HC):
                xa = f32v(o_x1t + h * FT, FT)
                fa = f32v(o_scr + h * FT, FT)
                la = bf16v(o_ln1, 4096)[:, h * SUP + tc0: h * SUP + tc0 + FT]
                P.op("vector", lambda eng, a=fa, b=xa: eng.tensor_tensor(
                         a, b, ps[2][:, 0:FT], OP.mult),
                     reads=("x1t", "ps2"), writes=("scr",), tag=f"l1a{st}{ts}")
                P.op("vector", lambda eng, a=fa: eng.tensor_tensor(
                         a, a, ps[3][:, 0:FT], OP.subtract),
                     reads=("scr", "ps3"), writes=("scr",), tag=f"l1b{st}{ts}")
                P.op("vector", lambda eng, a=la, b=fa: eng.tensor_copy(a, b),
                     reads=("scr",), writes=(f"ln1_{h}",), tag=f"l1c{st}{ts}")
            for k in range(HC):
                fa = f32v(o_scr + k * FT, FT)
                MM(ps[4][0:8, 0:FT], t_gw[:, k * 8:(k + 1) * 8], fa,
                   k == 0, k == HC - 1, ("scr", "gw"), ("ps4",), tag=f"gl{st}{ts}")
            P.op("scalar", lambda eng: eng.copy(
                     t_lg[0:8, 0:FT], ps[4][0:8, 0:FT]),
                 reads=("ps4",), writes=("lg",), tag=f"gle{st}{ts}")
            W4 = FT // 128
            for w in range(W4):
                P.op("tensor", lambda eng, w=w: eng.transpose(
                         ps[5][:, w * 8:(w + 1) * 8],
                         t_lg[0:8, w * 128:(w + 1) * 128], t_id[0:8, 0:8]),
                     reads=("lg", "ident"), writes=("ps5",), tag=f"rt1{st}{ts}{w}")
            for w in range(W4):
                P.op("vector", lambda eng, w=w: eng.reduce_max(
                         t_rsc[:, 32 + w:33 + w], ps[5][:, w * 8:(w + 1) * 8], AX.X),
                     reads=("ps5",), writes=("rtop",), tag=f"rt3{st}{ts}{w}")
            P.op("vector", lambda eng: eng.tensor_copy(
                     t_spc[0:1, 0:256], t_spc[0:1, 256:512]),
                 reads=("rtop",), writes=("rtop", "spc"), tag=f"rt3b{st}{ts}")
            P.op("scalar", lambda eng: eng.activation(
                     t_rsc[:, 36:40], t_rsc[:, 32:36], AF.Sigmoid),
                 reads=("rtop",), writes=("rsco",), tag=f"rt4{st}{ts}")
            P.op("scalar", lambda eng: eng.copy(
                     t_spc[0:1, 256:512], t_spc[0:1, 0:256]),
                 reads=("rsco",), writes=("rsco", "spc"), tag=f"rt4b{st}{ts}")
            for w in range(W4):
                P.op("vector", lambda eng, w=w: eng.tensor_scalar(
                         t_rsc[:, 40 + 9 * w:48 + 9 * w], ps[5][:, w * 8:(w + 1) * 8],
                         t_rsc[:, 32 + w:33 + w], None, OP.is_ge),
                     reads=("ps5", "rtop"), writes=("req",), tag=f"rt5{st}{ts}{w}")
            for w in range(W4):
                P.op("vector", lambda eng, w=w: eng.tensor_scalar(
                         t_rsc[:, 40 + 9 * w:48 + 9 * w], t_rsc[:, 40 + 9 * w:48 + 9 * w],
                         t_rsc[:, 36 + w:37 + w], None, OP.mult),
                     reads=("req", "rsco"), writes=("req",), tag=f"rt6{st}{ts}{w}")
            for w in range(W4):
                P.op("vector", lambda eng, w=w: eng.tensor_copy(
                         t_rsc[:, 48 + 9 * w:49 + 9 * w], t_rsc[:, 36 + w:37 + w]),
                     reads=("rsco",), writes=("req",), tag=f"rt7{st}{ts}{w}")
            P.op("vector", lambda eng: eng.tensor_copy(
                     t_spc[0:1, 256:512], t_spc[0:1, 0:256]),
                 reads=("req",), writes=("req", "spc"), tag=f"rt7b{st}{ts}")
            for w in range(W4):
                P.op("tensor", lambda eng, w=w: eng.transpose(
                         ps[6][0:9, w * 128:(w + 1) * 128],
                         t_rsc[:, 40 + 9 * w:49 + 9 * w], t_id[:, 0:128]),
                     reads=("req", "ident"), writes=("ps6",), tag=f"rt8{st}{ts}{w}")
            P.op("vector", lambda eng, tc0=tc0: eng.tensor_copy(
                     t_sm[0:9, tc0:tc0 + FT], ps[6][0:9, 0:FT]),
                 reads=("ps6",), writes=("smr",), tag=f"rt9{st}{ts}")
            if debug and st == 0 and ts == 0:
                tap("rsc0", t_rsc[:, 0:96], ("rtop", "rsco", "req"), (128, 96))
                tap("lg0", t_lg[0:8, 0:FT], ("lg",), (8, FT))

        P.op("vector", lambda eng: eng.tensor_copy(t_smb[0:9, :], t_sm[0:9, :]),
             reads=("smr",), writes=("smb",), tag=f"smcast{st}")

        for ei in range(9):
            routed = ei < 8
            if routed:
                for ts in range(NTS):
                    tc0 = ts * FT
                    MM(ps[7][:, 0:FT], t_sel[0:9, ei * 128:(ei + 1) * 128],
                       t_smb[0:9, tc0:tc0 + FT],
                       True, True, ("smb", "sel"), ("ps7",), tag=f"smb{st}_{ei}")
                    for h in range(HC):
                        la = bf16v(o_ln1, 4096)[:, h * SUP + tc0: h * SUP + tc0 + FT]
                        xa = bf16v(o_xs, 4096)[:, h * SUP + tc0: h * SUP + tc0 + FT]
                        P.op("vector", lambda eng, a=xa, b=la: eng.tensor_tensor(
                                 a, b, ps[7][:, 0:FT], OP.mult),
                             reads=(f"ln1_{h}", "ps7"), writes=(f"xs{h}",),
                             tag=f"xs{st}_{ei}_{h}")
            for ts in range(NTS):
                tc0 = ts * FT
                for q in range(4):
                    sidx = q % 2
                    i0 = q * 512
                    if routed:
                        gsrc, rg = rtg_b[ei, :, i0:i0 + 512], (f"wg{sidx}", f"wcg{ei}")
                        usrc, ru = rtu_b[ei, :, i0:i0 + 512], (f"wu{sidx}", f"wcu{ei}")
                        dsrc, rd2 = rtd_b[ei, i0:i0 + 512, :], (f"wd{sidx}", f"wcd{ei}")
                    else:
                        gsrc, rg = shg_b[:, i0:i0 + 512], (f"wg{sidx}", "wcg8")
                        usrc, ru = shu_b[:, i0:i0 + 512], (f"wu{sidx}", "wcu8")
                        dsrc, rd2 = shd_b[i0:i0 + 512, :], (f"wd{sidx}", "wcd8")
                    P.dma("sync", lambda eng, d_=bf16v(o_wg[sidx], 2048).rearrange("p (c i) -> p c i", c=8),
                          s_=gsrc.rearrange("(c p) i -> p c i", p=128): eng.dma_start(out=d_, in_=s_),
                          reads=rg[1:], writes=rg[:1], tag=f"lwg{st}_{ei}_{ts}_{q}")
                    P.dma("sync", lambda eng, d_=bf16v(o_wu[sidx], 2048).rearrange("p (c i) -> p c i", c=8),
                          s_=usrc.rearrange("(c p) i -> p c i", p=128): eng.dma_start(out=d_, in_=s_),
                          reads=ru[1:], writes=ru[:1], tag=f"lwu{st}_{ei}_{ts}_{q}")
                    P.dma("sync", lambda eng, d_=bf16v(o_wd[sidx], 2048).rearrange("p (c h) -> p c h", c=4),
                          s_=dsrc.rearrange("(c p) h -> p c h", p=128): eng.dma_start(out=d_, in_=s_),
                          reads=rd2[1:], writes=rd2[:1], tag=f"lwd{st}_{ei}_{ts}_{q}")
                    for il in range(4):
                        bg = (il % 2) * 2
                        bu = bg + 1
                        for k in range(HC):
                            lhs = bf16v(o_wg[sidx], 2048)[:, k * 512 + il * 128:
                                                          k * 512 + (il + 1) * 128]
                            if routed:
                                rhs = bf16v(o_xs, 4096)[:, k * SUP + tc0:
                                                        k * SUP + tc0 + FT]
                                rr = f"xs{k}"
                            else:
                                rhs = bf16v(o_ln1, 4096)[:, k * SUP + tc0:
                                                         k * SUP + tc0 + FT]
                                rr = f"ln1_{k}"
                            MM(ps[bg][:, 0:FT], lhs, rhs, k == 0, k == HC - 1,
                               (f"wg{sidx}", rr), (f"ps{bg}",), tag=f"g{ei}{ts}{q}{il}")
                        for k in range(HC):
                            lhs = bf16v(o_wu[sidx], 2048)[:, k * 512 + il * 128:
                                                          k * 512 + (il + 1) * 128]
                            if routed:
                                rhs = bf16v(o_xs, 4096)[:, k * SUP + tc0:
                                                        k * SUP + tc0 + FT]
                                rr = f"xs{k}"
                            else:
                                rhs = bf16v(o_ln1, 4096)[:, k * SUP + tc0:
                                                         k * SUP + tc0 + FT]
                                rr = f"ln1_{k}"
                            MM(ps[bu][:, 0:FT], lhs, rhs, k == 0, k == HC - 1,
                               (f"wu{sidx}", rr), (f"ps{bu}",), tag=f"u{ei}{ts}{q}{il}")
                        tpar = il % 2
                        tsa = t_tss[:, tpar * 512:tpar * 512 + FT]
                        P.op("scalar", lambda eng, a=tsa, bg=bg: eng.activation(
                                 a, ps[bg][:, 0:FT], AF.Silu),
                             reads=(f"ps{bg}",), writes=(f"tss{tpar}",),
                             tag=f"sg{ei}{ts}{q}{il}")
                        aa = bf16v(o_act[sidx], 1024)[:, il * FT:(il + 1) * FT]
                        P.op("vector", lambda eng, a=aa, b=tsa, bu=bu:
                                 eng.tensor_tensor(a, b, ps[bu][:, 0:FT], OP.mult),
                             reads=(f"tss{tpar}", f"ps{bu}"), writes=(f"act{sidx}_{il}",),
                             tag=f"am{ei}{ts}{q}{il}")
                    for hg in range(2):
                        for hh in range(4):
                            h = hg * 4 + hh
                            bank = 4 + hh
                            for il in range(4):
                                lhs = bf16v(o_wd[sidx], 2048)[:, il * 1024 + h * 128:
                                                              il * 1024 + (h + 1) * 128]
                                rhs = bf16v(o_act[sidx], 1024)[:, il * FT:(il + 1) * FT]
                                MM(ps[bank][:, 0:FT], lhs, rhs, il == 0, il == 3,
                                   (f"wd{sidx}", f"act{sidx}_{il}"), (f"ps{bank}",),
                                   tag=f"dn{ei}{ts}{q}{h}")
                        for hh in range(4):
                            h = hg * 4 + hh
                            bank = 4 + hh
                            fa = fac_slice(h, tc0, FT)
                            P.op("vector", lambda eng, a=fa, bk=bank:
                                     eng.tensor_tensor(a, a, ps[bk][:, 0:FT], OP.add),
                                 reads=(f"ps{bank}", f"fac{h}"), writes=(f"fac{h}",),
                                 tag=f"fa{ei}{ts}{q}{h}")

        for ts in range(NTS):
            tc0 = ts * FT
            x1src = x1h.rearrange("(h p) t -> p h t", p=128)[:, :, s0 + tc0:s0 + tc0 + FT]
            P.dma("sync", lambda eng, sp=x1src: eng.dma_start(
                      out=f32v(o_x1t, 4096).rearrange("p (h t) -> p h t", h=8), in_=sp),
                  reads=tuple(f"x1h{c}" for c in range(NCH)),
                  writes=("x1t",), tag=f"ldx1b_{st}_{ts}")
            fsl = [(fac_slice(h, tc0, FT), f"fac{h}") for h in range(HC)]
            ln_stats(fsl, lambda k: (f32v(o_scr + k * FT, FT), "scr"),
                     ps_sum=(ps[0], "ps0"), ps_sq=(ps[1], "ps1"),
                     n_feat=D, tc=FT, regs_src=None, inst=f"l2_{st}_{ts}")
            ln_rows((ps[0], "ps0"), (ps[1], "ps1"), D, FT, f"l2_{st}_{ts}",
                    psr=(ps[2], "ps2"), psm=(ps[3], "ps3"))
            for h in range(HC):
                fa = fac_slice(h, tc0, FT)
                xa = f32v(o_x1t + h * FT, FT)
                oa = f32v(o_out + h * FT, FT)
                P.op("vector", lambda eng, a=oa, b=fa: eng.tensor_tensor(
                         a, b, ps[2][:, 0:FT], OP.mult),
                     reads=(f"fac{h}", "ps2"), writes=("outb",), tag=f"l2a{st}{ts}")
                P.op("vector", lambda eng, a=oa: eng.tensor_tensor(
                         a, a, ps[3][:, 0:FT], OP.subtract),
                     reads=("outb", "ps3"), writes=("outb",), tag=f"l2b{st}{ts}")
                P.op("vector", lambda eng, a=oa, b=xa: eng.tensor_tensor(
                         a, a, b, OP.add),
                     reads=("outb", "x1t"), writes=("outb",), tag=f"l2r{st}{ts}")
            odst = outT.rearrange("(h p) t -> p h t", p=128)[:, :, s0 + tc0:s0 + tc0 + FT]
            P.dma("sync", lambda eng, dp=odst: eng.dma_start(
                      out=dp, in_=f32v(o_out, 4096).rearrange("p (h t) -> p h t", h=8)),
                  reads=("outb",), writes=(f"outh{st}{ts}",), tag=f"sto{st}{ts}")

    P.analyze()
    bodyf = P.emit(nc, sems)

    with nc.Block() as block:
        @block.sync
        def _(eng):
            bodyf(eng, "sync")
            eng.wait_ge(sems["qsync"], 16 * P.final_qcnt["qsync"])
            eng.wait_ge(sems["qgps"], 16 * P.final_qcnt["qgps"])

        @block.gpsimd
        def _(eng):
            bodyf(eng, "gpsimd")

        @block.vector
        def _(eng):
            bodyf(eng, "vector")

        @block.scalar
        def _(eng):
            bodyf(eng, "scalar")

        @block.tensor
        def _(eng):
            bodyf(eng, "tensor")

    ctx.close()
    return nc


_NC = None
DEBUG = False


def _host_inputs(inputs):
    g = {}
    for k, v in inputs.items():
        a = np.asarray(v)
        if a.dtype != np.float32:
            a = a.astype(np.float32)
        g[k] = a
    x = g["L0_attn_x"]
    base = {}
    base["wiT"] = np.ascontiguousarray(g["L0_attn_in_proj_w"].T)
    base["woT"] = np.ascontiguousarray(g["L0_attn_out_proj_w"].T)
    base["xpT"] = np.ascontiguousarray(g["L0_attn_x_proj_w"].T)
    base["Als"] = np.ascontiguousarray(
        g["L0_attn_A_log"].reshape(16, 128, 16).transpose(1, 0, 2).reshape(128, 256))
    base["wdt"] = np.ascontiguousarray(g["L0_attn_dt_proj_w"].reshape(16, 128).T)
    base["bdt"] = np.ascontiguousarray(g["L0_attn_dt_proj_b"].reshape(16, 128).T)
    base["cw"] = np.ascontiguousarray(
        g["L0_attn_conv_w"].reshape(16, 128, 4).transpose(1, 0, 2).reshape(128, 64))
    base["Dv"] = np.ascontiguousarray(g["L0_attn_D"].reshape(16, 128).T)
    base["nw"] = np.ascontiguousarray(g["L0_attn_norm_w"].reshape(8, 128).T)
    base["nb"] = np.ascontiguousarray(g["L0_attn_norm_b"].reshape(8, 128).T)
    base["gw"] = np.ascontiguousarray(g["L0_ffn_gate_w"])
    base["ones34"] = np.ones((34, 128), np.float32)
    sel = np.zeros((33, 33, 128), np.float32)
    for jj in range(33):
        sel[jj, jj, :] = 1.0
    base["selmat"] = np.ascontiguousarray(sel.transpose(1, 0, 2).reshape(33, 33 * 128))
    base["ident"] = np.eye(128, dtype=np.float32)
    base["onescol"] = np.ones((128, 1), np.float32)
    base["shg"] = np.ascontiguousarray(g["L0_ffn_sh_gate"])
    base["shu"] = np.ascontiguousarray(g["L0_ffn_sh_up"])
    base["shd"] = np.ascontiguousarray(g["L0_ffn_sh_down"])
    base["rtg"] = np.ascontiguousarray(g["L0_ffn_rt_gate"])
    base["rtu"] = np.ascontiguousarray(g["L0_ffn_rt_up"])
    base["rtd"] = np.ascontiguousarray(g["L0_ffn_rt_down"])
    in_maps = []
    for c in range(8):
        m = dict(base)
        m["xT"] = np.ascontiguousarray(x[c].T)
        in_maps.append(m)
    return in_maps


def kernel(**inputs):
    global _NC
    if _NC is None:
        _NC = build_nc()
    in_maps = _host_inputs(inputs)
    res = run_bass_kernel_spmd(_NC, in_maps, core_ids=list(range(8)))
    out = np.stack([np.asarray(res.results[c]["outT"]).T for c in range(8)], axis=0)
    return np.ascontiguousarray(out.astype(np.float32))


# version 07
# speedup vs baseline: 1.1590x; 1.1590x over previous
import numpy as np
import concourse.bass as bass
import concourse.mybir as mybir
from concourse.bass_utils import run_bass_kernel_spmd

F32 = mybir.dt.float32
BF16 = mybir.dt.bfloat16
AF = mybir.ActivationFunctionType
OP = mybir.AluOpType
AX = mybir.AxisListType

D = 1024
DI = 2048
T = 2048
DS = 16
HC = 8
IC = 16
TCM = 256
NCH = T // TCM
SUP = 1024
FT = 512
NST = T // SUP
NTS = SUP // FT
EPS = 1e-5

ENGINES = ("tensor", "vector", "scalar", "gpsimd", "sync")
QUEUES = ("qsync", "qgps")


class Plan:

    def __init__(self):
        self.ops = []
        self.fences = []

    def op(self, stream, fn, reads=(), writes=(), queue=None, tag=""):
        assert stream in ENGINES
        self.ops.append(dict(
            stream=stream, fn=fn, reads=tuple(reads), writes=tuple(writes),
            queue=queue, tag=tag, deps=[], signal=False, sig=None, waits=None,
        ))

    def dma(self, stream, fn, reads=(), writes=(), tag=""):
        q = "qgps" if stream == "gpsimd" else "qsync"
        self.op(stream, fn, reads, writes, queue=q, tag=tag)

    def fence(self):
        self.fences.append(len(self.ops))

    PARANOID = False

    def analyze(self):
        last_w = {}
        readers = {}
        fence_set = set(self.fences)
        last_of = {}
        fence_deps = None
        fence_waiting = set()
        for i, o in enumerate(self.ops):
            if i in fence_set:
                fence_deps = dict(last_of)
                fence_waiting = set(ENGINES)
            deps = set()
            if fence_deps is not None and o["stream"] in fence_waiting:
                for s, j in fence_deps.items():
                    deps.add(j)
                fence_waiting.discard(o["stream"])
                if not fence_waiting:
                    fence_deps = None
            for r in o["reads"]:
                if r in last_w:
                    deps.add(last_w[r])
            for r in o["writes"]:
                if r in last_w:
                    deps.add(last_w[r])
                for j in readers.get(r, ()):
                    deps.add(j)
            own = o["queue"] or o["stream"]
            deps.discard(i)
            dd = []
            for j in deps:
                p = self.ops[j]
                psrc = p["queue"] or p["stream"]
                if psrc == own and p["queue"] is None and o["queue"] is None:
                    continue
                if p["queue"] is not None and p["queue"] == o.get("queue") \
                        and p["stream"] == o["stream"]:
                    continue
                if p["queue"] is None and p["stream"] == o["stream"]:
                    continue
                dd.append(j)
            o["deps"] = dd
            for j in dd:
                self.ops[j]["signal"] = True
            for r in o["reads"]:
                readers.setdefault(r, []).append(i)
            for r in o["writes"]:
                last_w[r] = i
                readers[r] = []
            last_of[own] = i
        cnt = {s: 0 for s in ENGINES}
        qcnt = {q: 0 for q in QUEUES}
        for o in self.ops:
            if o["queue"] is not None:
                qcnt[o["queue"]] += 1
                o["sig"] = qcnt[o["queue"]]
                o["signal"] = True
            elif o["signal"]:
                cnt[o["stream"]] += 1
                o["sig"] = cnt[o["stream"]]
        self.final_qcnt = qcnt
        clocks = {s: {} for s in ENGINES}

        def merge(c, snap):
            for k, v in snap.items():
                if c.get(k, 0) < v:
                    c[k] = v

        for o in self.ops:
            s = o["stream"]
            c = clocks[s]
            waits = {}
            for j in o["deps"]:
                p = self.ops[j]
                src = p["queue"] or p["stream"]
                v = p["sig"]
                assert v is not None
                if not Plan.PARANOID and c.get(src, 0) >= v:
                    continue
                if waits.get(src, 0) < v:
                    waits[src] = v
            for src, v in waits.items():
                pass
            for j in o["deps"]:
                p = self.ops[j]
                src = p["queue"] or p["stream"]
                if src in waits and p["sig"] <= waits[src]:
                    merge(c, p.get("snap", {}))
            for src, v in waits.items():
                if c.get(src, 0) < v:
                    c[src] = v
            o["waits"] = sorted(waits.items())
            if o["signal"]:
                src = o["queue"] or s
                snap = dict(c)
                snap[src] = o["sig"]
                if o["queue"] is None:
                    snap[s] = o["sig"]
                o["snap"] = snap

    def emit(self, nc, sems):
        engine_ops = {s: [] for s in ENGINES}
        for o in self.ops:
            engine_ops[o["stream"]].append(o)

        def body(eng, s):
            for o in engine_ops[s]:
                for src, v in o["waits"]:
                    val = v * 16 if src in QUEUES else v
                    eng.wait_ge(sems[src], val)
                inst = o["fn"](eng)
                if o["queue"] is not None:
                    inst.then_inc(sems[o["queue"]], 16)
                elif o["signal"]:
                    inst.then_inc(sems[o["stream"]], 1)
        return body


def ceil_div(a, b):
    return -(-a // b)


kernel_scan_stub = False


def build_nc(debug=False, nch=NCH, do_ffn=True):
    nc = bass.Bass("TRN2", target_bir_lowering=False)
    dbg_outs = {}

    dr = {}
    def din(name, shape):
        dr[name] = nc.dram_tensor(name, list(shape), F32, kind="ExternalInput").ap()
    din("xT", (D, T))
    din("wiT", (D, DI))
    din("woT", (DI, D))
    din("xpT", (DI, 33))
    din("Als", (128, 256))
    din("wdt", (128, 16))
    din("bdt", (128, 16))
    din("cw", (128, 64))
    din("Dv", (128, 16))
    din("nw", (128, 8))
    din("nb", (128, 8))
    din("gw", (D, 8))
    din("ones34", (34, 128))
    din("selmat", (33, 33 * 128))
    din("ident", (128, 128))
    din("onescol", (128, 1))
    din("shg", (D, DI))
    din("shu", (D, DI))
    din("shd", (DI, D))
    din("rtg", (8, D, DI))
    din("rtu", (8, D, DI))
    din("rtd", (8, DI, D))
    x1h = nc.dram_tensor("x1h", [D, T], F32, kind="Internal").ap()
    rtg_b = nc.dram_tensor("rtg_b", [8, D, DI], BF16, kind="Internal").ap()
    rtu_b = nc.dram_tensor("rtu_b", [8, D, DI], BF16, kind="Internal").ap()
    rtd_b = nc.dram_tensor("rtd_b", [8, DI, D], BF16, kind="Internal").ap()
    shg_b = nc.dram_tensor("shg_b", [D, DI], BF16, kind="Internal").ap()
    shu_b = nc.dram_tensor("shu_b", [D, DI], BF16, kind="Internal").ap()
    shd_b = nc.dram_tensor("shd_b", [DI, D], BF16, kind="Internal").ap()
    outT = nc.dram_tensor("outT", [D, T], F32, kind="ExternalOutput").ap()

    import contextlib
    ctx = contextlib.ExitStack()
    e = ctx.enter_context
    t_Als = e(nc.sbuf_tensor("t_Als", [128, 256], F32))
    t_wdt = e(nc.sbuf_tensor("t_wdt", [128, 16], F32))
    t_bdt = e(nc.sbuf_tensor("t_bdt", [128, 16], F32))
    t_cw = e(nc.sbuf_tensor("t_cw", [128, 64], F32))
    t_Dv = e(nc.sbuf_tensor("t_Dv", [128, 16], F32))
    t_nw = e(nc.sbuf_tensor("t_nw", [128, 8], F32))
    t_nb = e(nc.sbuf_tensor("t_nb", [128, 8], F32))
    t_gw = e(nc.sbuf_tensor("t_gw", [128, 64], F32))
    t_ones = e(nc.sbuf_tensor("t_ones", [34, 128], F32))
    t_sel = e(nc.sbuf_tensor("t_sel", [33, 33 * 128], BF16))
    t_id = e(nc.sbuf_tensor("t_id", [128, 128], F32))
    t_oc = e(nc.sbuf_tensor("t_oc", [128, 1], F32))
    t_states = e(nc.sbuf_tensor("t_states", [128, 256], F32))
    t_srow = e(nc.sbuf_tensor("t_srow", [1, 1536], F32))
    t_sm = e(nc.sbuf_tensor("t_sm", [16, SUP], F32))
    t_smb = e(nc.sbuf_tensor("t_smb", [16, SUP], BF16))
    t_lg = e(nc.sbuf_tensor("t_lg", [16, FT], F32))
    t_rsc = e(nc.sbuf_tensor("t_rsc", [128, 96], F32))
    t_spc = e(nc.sbuf_tensor("t_spc", [1, 512], F32))
    t_projc = e(nc.sbuf_tensor("t_projc", [33, 256], F32))
    t_projb = e(nc.sbuf_tensor("t_projb", [33, 256], BF16))
    t_dt1 = e(nc.sbuf_tensor("t_dt1", [128, 256], F32))
    t_tss = e(nc.sbuf_tensor("t_tss", [128, 2048], BF16))

    BIGW = 44200
    big = e(nc.sbuf_tensor("big", [128, BIGW], F32))

    ps = [e(nc.psum_tensor(f"ps{i}", [128, 512], F32)) for i in range(8)]

    sems = {}
    for s in ENGINES + QUEUES:
        sems[s] = e(nc.semaphore(f"sem_{s}"))

    off = [0]
    def alloc(w):
        o = off[0]
        off[0] += w
        assert off[0] <= BIGW, f"big overflow {off[0]}"
        return o
    viewlog = []
    def f32v(o, w):
        viewlog.append((len(P.ops) if 'P' in dir() else -1, o, w, 'f32'))
        return big[:, o:o + w]
    def bf16v(o, w32):
        viewlog.append((len(P.ops) if 'P' in dir() else -1, o, w32, 'bf16'))
        return big[:, o:o + w32].bitcast(BF16)
    nc._viewlog = viewlog

    o_wi = alloc(8192)
    o_wo = alloc(8192)
    o_xp = alloc(272)
    o_xc = alloc(2048)
    o_lnx = alloc(2048)
    o_h0b = alloc(1024)
    o_hin = alloc(16 * 259)
    o_hb = alloc(2048)
    o_dt = alloc(2048)
    o_y = alloc(4096)
    o_yb = alloc(2048)
    o_mo = alloc(2048)
    o_sq = alloc(2048)
    o_sc = alloc(5 * 256)
    o_sall = alloc(2048)
    o_Bs = [alloc(128), alloc(128)]
    o_Cs = [alloc(128), alloc(128)]
    o_cacc = o_sq
    m_end = off[0]
    assert m_end <= BIGW

    off[0] = 0
    o_x1t = alloc(4096)
    o_fac = alloc(4096)
    o_fac2 = alloc(4096)
    o_ln1 = alloc(4096)
    o_scr = alloc(4096)
    o_wg = [alloc(2048), alloc(2048)]
    o_wu = [alloc(2048), alloc(2048)]
    o_wd = [alloc(2048), alloc(2048)]
    o_xs = alloc(4096)
    o_act = [alloc(1024), alloc(1024)]
    o_out = alloc(4096)
    f_end = off[0]
    assert f_end <= BIGW

    def fac_slice(h, c0, w):
        base = h * SUP + c0
        assert base // 4096 == (base + w - 1) // 4096
        o = o_fac if base < 4096 else o_fac2
        return f32v(o + base % 4096, w)

    P = Plan()

    def ld(dst_ap, src_ap, regs, cast=False, tag=""):
        s = "gpsimd" if cast else "sync"
        P.dma(s, lambda eng, d=dst_ap, sp=src_ap: eng.dma_start(out=d, in_=sp),
              reads=(), writes=regs, tag=tag)

    ld(t_Als[:], dr["Als"], ("Als",))
    ld(t_wdt[:], dr["wdt"], ("wdt",))
    ld(t_bdt[:], dr["bdt"], ("bdt",))
    ld(t_cw[:], dr["cw"], ("cw",))
    ld(t_Dv[:], dr["Dv"], ("Dv",))
    ld(t_nw[:], dr["nw"], ("nw",))
    ld(t_nb[:], dr["nb"], ("nb",))
    ld(t_gw[:].rearrange("p (c e) -> p c e", c=8), dr["gw"].rearrange("(c p) e -> p c e", p=128), ("gw",))
    ld(t_ones[:], dr["ones34"], ("ones",))
    ld(t_sel[:], dr["selmat"], ("sel",), cast=True)
    ld(t_id[:], dr["ident"], ("ident",))
    ld(t_oc[:], dr["onescol"], ("onescol",))
    ld(bf16v(o_wi, 8192).rearrange("p (c i) -> p c i", c=8), dr["wiT"].rearrange("(c p) i -> p c i", p=128),
       ("wi",), cast=True)
    ld(bf16v(o_wo, 8192).rearrange("p (c h) -> p c h", c=16), dr["woT"].rearrange("(c p) h -> p c h", p=128),
       ("wo",), cast=True)
    ld(bf16v(o_xp, 264)[:, 0:528].rearrange("p (c m) -> p c m", c=16), dr["xpT"].rearrange("(c p) m -> p c m", p=128),
       ("xp",), cast=True)

    for ee in range(8):
        ld(rtg_b[ee], dr["rtg"][ee], (f"wcg{ee}",), cast=True, tag=f"pcg{ee}")
        ld(rtu_b[ee], dr["rtu"][ee], (f"wcu{ee}",), cast=True, tag=f"pcu{ee}")
        ld(rtd_b[ee], dr["rtd"][ee], (f"wcd{ee}",), cast=True, tag=f"pcd{ee}")
    ld(shg_b, dr["shg"], ("wcg8",), cast=True, tag="pcg8")
    ld(shu_b, dr["shu"], ("wcu8",), cast=True, tag="pcu8")
    ld(shd_b, dr["shd"], ("wcd8",), cast=True, tag="pcd8")

    P.op("scalar", lambda eng: eng.activation(t_Als[:], t_Als[:], AF.Exp),
         reads=("Als",), writes=("Als",), tag="Aexp")
    P.op("vector", lambda eng: eng.tensor_scalar_mul(t_Als[:], t_Als[:], -1.0),
         reads=("Als",), writes=("Als",), tag="Aneg")
    P.op("vector", lambda eng: eng.memset(t_states[:], 0.0),
         writes=("states",), tag="st0")
    P.op("vector", lambda eng: eng.memset(t_spc[:], 0.0), writes=("spc",), tag="spc0")
    for j in range(IC):
        o = o_hin + j * 259
        P.op("vector", lambda eng, o=o: eng.memset(f32v(o, 3), 0.0),
             writes=(f"hin{j}",), tag="halo0")

    def tap(name, ap, regions, shape2):
        if not debug:
            return
        if ap.dtype != F32 and debug != "all":
            return
        o = nc.dram_tensor("dbg_" + name, list(shape2), F32, kind="ExternalOutput").ap()
        dbg_outs[name] = o
        cast = ap.dtype != F32
        s = "gpsimd" if cast else "sync"
        P.dma(s, lambda eng, a=ap, oo=o: eng.dma_start(out=oo, in_=a),
              reads=tuple(regions), writes=("dbg_" + name,), tag="tap_" + name)

    def MM(psl, lhsT, rhs, start, stop, reads, wr, tag=""):
        P.op("tensor",
             lambda eng, a=psl, b=lhsT, c=rhs, s1=start, s2=stop:
                 eng.matmul(a, b, c, start=s1, stop=s2),
             reads=reads, writes=wr, tag=tag)

    def ln_stats(src_slices, sq_dst, ps_sum, ps_sq, n_feat, tc, regs_src, inst, do_sq=True):
        nk = len(src_slices)
        for k, (apx, rg) in enumerate(src_slices):
            a2, rg2 = sq_dst(k)
            P.op("scalar", lambda eng, o=a2, i=apx: eng.activation(o, i, AF.Square),
                 reads=(rg,), writes=(rg2,), tag=f"sq{inst}")
        for k, (apx, rg) in enumerate(src_slices):
            MM(ps_sum[0][0:1, 0:tc], t_oc[:], apx, k == 0, k == nk - 1,
               (rg, "onescol"), (ps_sum[1],), tag=f"sum{inst}")
        for k in range(nk):
            a2, rg2 = sq_dst(k)
            MM(ps_sq[0][0:1, 0:tc], t_oc[:], a2, k == 0, k == nk - 1,
               (rg2, "onescol"), (ps_sq[1],), tag=f"ssq{inst}")

    rowc = [0]
    def ln_rows(ps_sum, ps_sq, n_feat, tc, inst, psr, psm):
        c0 = 0
        mu = t_srow[0:1, c0:c0 + tc]
        m2 = t_srow[0:1, 512:512 + tc]
        rs = t_srow[0:1, 1024:1024 + tc]
        rmu = f"r_mu{inst}"; rm2 = f"r_m2{inst}"; rrs = f"r_rs{inst}"
        P.op("vector", lambda eng: eng.tensor_scalar_mul(mu, ps_sum[0][0:1, 0:tc], 1.0 / n_feat),
             reads=(ps_sum[1],), writes=(rmu,), tag=f"mu{inst}")
        P.op("vector", lambda eng: eng.tensor_scalar_mul(m2, ps_sq[0][0:1, 0:tc], 1.0 / n_feat),
             reads=(ps_sq[1],), writes=(rm2,), tag=f"m2{inst}")
        P.op("vector", lambda eng: eng.tensor_tensor(rs, mu, mu, OP.mult),
             reads=(rmu,), writes=(rrs,), tag=f"musq{inst}")
        P.op("vector", lambda eng: eng.tensor_tensor(m2, m2, rs, OP.subtract),
             reads=(rm2, rrs), writes=(rm2,), tag=f"var{inst}")
        P.op("vector", lambda eng: eng.tensor_scalar_add(m2, m2, EPS),
             reads=(rm2,), writes=(rm2,), tag=f"veps{inst}")
        P.op("scalar", lambda eng: eng.activation(m2, m2, AF.Ln),
             reads=(rm2,), writes=(rm2,), tag=f"lnv{inst}")
        P.op("vector", lambda eng: eng.tensor_scalar_mul(m2, m2, -0.5),
             reads=(rm2,), writes=(rm2,), tag=f"vneg{inst}")
        P.op("scalar", lambda eng: eng.activation(rs, m2, AF.Exp),
             reads=(rm2,), writes=(rrs,), tag=f"rstd{inst}")
        P.op("vector", lambda eng: eng.tensor_tensor(mu, mu, rs, OP.mult),
             reads=(rmu, rrs), writes=(rmu,), tag=f"mup{inst}")
        MM(psr[0][:, 0:tc], t_ones[0:1, :], rs, True, True,
           (rrs, "ones"), (psr[1],), tag=f"bcr{inst}")
        MM(psm[0][:, 0:tc], t_ones[0:1, :], mu, True, True,
           (rmu, "ones"), (psm[1],), tag=f"bcm{inst}")

    for c in range(nch):
        cw0 = c * TCM
        xsrc = dr["xT"].rearrange("(h p) t -> p h t", p=128)[:, :, cw0:cw0 + TCM]
        P.dma("sync", lambda eng, sp=xsrc: eng.dma_start(out=f32v(o_xc, 2048).rearrange("p (h t) -> p h t", h=8), in_=sp),
              writes=("xc",), tag=f"ldx{c}")

        xsl = [(f32v(o_xc + h * TCM, TCM), "xc") for h in range(HC)]
        ln_stats(xsl, lambda k: (f32v(o_sq + k * TCM, TCM), "sq"),
                 ps_sum=(ps[0], "ps0"), ps_sq=(ps[1], "ps1"),
                 n_feat=D, tc=TCM, regs_src=None, inst=f"l0c{c}")
        ln_rows((ps[0], "ps0"), (ps[1], "ps1"), D, TCM, f"l0c{c}",
                psr=(ps[2], "ps2"), psm=(ps[3], "ps3"))
        for h in range(HC):
            xa = f32v(o_xc + h * TCM, TCM)
            la = f32v(o_lnx + h * TCM, TCM)
            P.op("vector", lambda eng, a=la, b=xa: eng.tensor_tensor(
                     a, b, ps[2][:, 0:TCM], OP.mult),
                 reads=("xc", "ps2"), writes=("lnx",), tag=f"l0a{c}")
            P.op("vector", lambda eng, a=la: eng.tensor_tensor(
                     a, a, ps[3][:, 0:TCM], OP.subtract),
                 reads=("lnx", "ps3"), writes=("lnx",), tag=f"l0b{c}")

        if debug and c == 0:
            tap("lnx0", f32v(o_lnx, 2048), ("lnx",), (128, 2048))
        lsl = [(f32v(o_lnx + h * TCM, TCM), "lnx") for h in range(HC)]
        ln_stats(lsl, lambda k: (f32v(o_sq + k * TCM, TCM), "sq"),
                 ps_sum=(ps[0], "ps0"), ps_sq=(ps[1], "ps1"),
                 n_feat=D, tc=TCM, regs_src=None, inst=f"lic{c}")
        ln_rows((ps[0], "ps0"), (ps[1], "ps1"), D, TCM, f"lic{c}",
                psr=(ps[2], "ps2"), psm=(ps[3], "ps3"))
        for h in range(HC):
            la = f32v(o_lnx + h * TCM, TCM)
            sqa = f32v(o_sq + h * TCM, TCM)
            ha = bf16v(o_h0b, 1024)[:, h * TCM:(h + 1) * TCM]
            P.op("vector", lambda eng, a=sqa, b=la: eng.tensor_tensor(
                     a, b, ps[2][:, 0:TCM], OP.mult),
                 reads=("lnx", "ps2"), writes=("sq",), tag=f"lia{c}")
            P.op("vector", lambda eng, a=sqa: eng.tensor_tensor(
                     a, a, ps[3][:, 0:TCM], OP.subtract),
                 reads=("sq", "ps3"), writes=("sq",), tag=f"lib{c}")
            P.op("vector", lambda eng, a=ha, b=sqa, h=h: eng.tensor_scalar(
                     a, b, t_nw[:, h:h + 1], t_nb[:, h:h + 1], OP.mult, OP.add),
                 reads=("sq", "nw", "nb"), writes=("h0b",), tag=f"lic{c}")

        if debug and c == 0:
            tap("h0b0", bf16v(o_h0b, 1024), ("h0b",), (128, 2048))
        for j in range(IC):
            bank = 4 + (j % 2)
            for k in range(HC):
                lhs = bf16v(o_wi, 8192)[:, k * 2048 + j * 128: k * 2048 + (j + 1) * 128]
                rhs = bf16v(o_h0b, 1024)[:, k * TCM:(k + 1) * TCM]
                MM(ps[bank][:, 0:TCM], lhs, rhs, k == 0, k == HC - 1,
                   ("wi", "h0b"), (f"ps{bank}",), tag=f"ip{c}_{j}")
            o = o_hin + j * 259
            if c > 0:
                P.op("vector", lambda eng, o=o: eng.tensor_copy(
                         f32v(o, 3), f32v(o + 256, 3)),
                     reads=(f"hin{j}",), writes=(f"hin{j}",), tag=f"halo{c}_{j}")
            P.op("scalar", lambda eng, o=o, b=bank: eng.copy(
                     f32v(o + 3, 256), ps[b][:, 0:TCM]),
                 reads=(f"ps{bank}",), writes=(f"hin{j}",), tag=f"ipe{c}_{j}")

        for j in range(IC):
            o = o_hin + j * 259
            acc = f32v(o_cacc + (j % 8) * TCM, TCM)
            racc = f"cacc{j % 8}"
            P.op("vector", lambda eng, a=acc, o=o, j=j: eng.tensor_scalar(
                     a, f32v(o + 0, 256), t_cw[:, j * 4:j * 4 + 1], None, OP.mult),
                 reads=(f"hin{j}", "cw"), writes=(racc,), tag=f"cv0{c}_{j}")
            for k in (1, 2, 3):
                P.op("vector", lambda eng, a=acc, o=o, j=j, k=k:
                         eng.scalar_tensor_tensor(
                             a, f32v(o + k, 256), t_cw[:, j * 4 + k:j * 4 + k + 1],
                             a, OP.mult, OP.add),
                     reads=(f"hin{j}", "cw", racc), writes=(racc,), tag=f"cv{k}{c}_{j}")
            hba = bf16v(o_hb, 2048)[:, j * TCM:(j + 1) * TCM]
            P.op("scalar", lambda eng, a=hba, b=acc: eng.activation(a, b, AF.Silu),
                 reads=(racc,), writes=(f"hb{j}",), tag=f"silu{c}_{j}")

        if debug and c == 0:
            tap("hin0", f32v(o_hin, 16 * 259), tuple(f"hin{j}" for j in range(IC)), (128, 16 * 259))
            tap("hb0", bf16v(o_hb, 2048), tuple(f"hb{j}" for j in range(IC)), (128, 4096))
        for k in range(IC):
            lhs = bf16v(o_xp, 264)[:, k * 33:(k + 1) * 33]
            rhs = bf16v(o_hb, 2048)[:, k * TCM:(k + 1) * TCM]
            MM(ps[6][0:33, 0:TCM], lhs, rhs, k == 0, k == IC - 1,
               ("xp", f"hb{k}"), ("ps6",), tag=f"xp{c}")
        P.op("scalar", lambda eng: eng.copy(t_projc[:], ps[6][0:33, 0:TCM]),
             reads=("ps6",), writes=("projc",), tag=f"xpe{c}")
        P.op("vector", lambda eng: eng.tensor_copy(t_projb[:], t_projc[:]),
             reads=("projc",), writes=("projb",), tag=f"xpb{c}")

        if debug and c == 0:
            tap("projc0", t_projc[:], ("projc",), (33, 256))
        MM(ps[7][:, 0:TCM], t_sel[0:33, 0:128], t_projb[0:33, :], True, True,
           ("projb", "sel"), ("ps7",), tag=f"dti{c}")
        for j in range(IC):
            P.op("scalar", lambda eng, j=j: eng.activation(
                     t_dt1[:], ps[7][:, 0:TCM], AF.Exp,
                     bias=t_bdt[:, j:j + 1], scale=t_wdt[:, j:j + 1]),
                 reads=("ps7", "wdt", "bdt"), writes=("dt1",), tag=f"dte{c}_{j}")
            P.op("scalar", lambda eng, j=j: eng.activation(
                     bf16v(o_dt, 2048)[:, j * TCM:(j + 1) * TCM], t_dt1[:], AF.Ln, bias=1.0),
                 reads=("dt1",), writes=(f"dt{j}",), tag=f"dtl{c}_{j}")

        if debug and c == 0:
            tap("dt0", f32v(o_dt, 4096), tuple(f"dt{j}" for j in range(IC)), (128, 4096))
        for d in range(DS):
            bB = 4 + (d % 2) * 2
            bC = bB + 1
            MM(ps[bB][:, 0:TCM], t_sel[0:33, (1 + d) * 128:(2 + d) * 128],
               t_projb[0:33, :], True, True, ("projb", "sel"), (f"ps{bB}",),
               tag=f"Bbc{c}_{d}")
            MM(ps[bC][:, 0:TCM], t_sel[0:33, (17 + d) * 128:(18 + d) * 128],
               t_projb[0:33, :], True, True, ("projb", "sel"), (f"ps{bC}",),
               tag=f"Cbc{c}_{d}")
            dp = d % 2
            P.op("scalar", lambda eng, bb=bB, dp=dp: eng.copy(
                     bf16v(o_Bs[dp], 128), ps[bb][:, 0:TCM]),
                 reads=(f"ps{bB}",), writes=(f"Bsb{dp}",), tag=f"Bev{c}_{d}")
            P.op("scalar", lambda eng, bc=bC, dp=dp: eng.copy(
                     bf16v(o_Cs[dp], 128), ps[bc][:, 0:TCM]),
                 reads=(f"ps{bC}",), writes=(f"Csb{dp}",), tag=f"Cev{c}_{d}")
            for j in range(IC):
                it = d * IC + j
                par = it % 2
                dAa = f32v(o_sc + par * TCM, TCM)
                bxa = f32v(o_sc + (2 + it % 3) * TCM, TCM)
                sa = bf16v(o_sall, 2048)[:, j * TCM:(j + 1) * TCM]
                rdA, rbx = (f"dA{par}", f"bx{it % 3}")
                rs_ = f"s16_{j}"
                col = j * 16 + d
                P.op("scalar", lambda eng, a=dAa, j=j, col=col: eng.activation(
                         a, bf16v(o_dt, 2048)[:, j * TCM:(j + 1) * TCM], AF.Exp,
                         scale=t_Als[:, col:col + 1]),
                     reads=(f"dt{j}", "Als"), writes=(rdA,), tag=f"dA{c}_{d}_{j}")
                hbj = bf16v(o_hb, 2048)[:, j * TCM:(j + 1) * TCM]
                P.op("gpsimd", lambda eng, a=bxa, b=hbj, dp=d % 2: eng.tensor_tensor(
                         a, b, bf16v(o_Bs[dp], 128), OP.mult),
                     reads=(f"hb{j}", f"Bsb{d % 2}"), writes=(rbx,), tag=f"bx{c}_{d}_{j}")
                P.op("vector", lambda eng, a=sa, d0=dAa, d1=bxa, col=col:
                         eng.tensor_tensor_scan(
                             a, d0, d1, t_states[:, col:col + 1], OP.mult, OP.add),
                     reads=(rdA, rbx, f"st{col}"), writes=(rs_,), tag=f"scan{c}_{d}_{j}")
                P.op("vector", lambda eng, a=sa, col=col: eng.tensor_copy(
                         t_states[:, col:col + 1], a[:, TCM - 1:TCM]),
                     reads=(rs_,), writes=(f"st{col}",), tag=f"stc{c}_{d}_{j}")
            s_regs = tuple(f"s16_{j}" for j in range(IC))
            sall3 = bf16v(o_sall, 2048).rearrange("p (r t) -> p r t", r=IC)
            y3 = f32v(o_y, 4096).rearrange("p (r t) -> p r t", r=IC)
            if d == 0:
                P.op("vector", lambda eng, dp=d % 2: eng.tensor_tensor(
                         y3, sall3,
                         bf16v(o_Cs[dp], 128).unsqueeze(1).broadcast_to([128, IC, TCM]),
                         OP.mult),
                     reads=s_regs + (f"Csb{d % 2}",), writes=("yall",),
                     tag=f"ro{c}_{d}")
            else:
                P.op("vector", lambda eng, dp=d % 2: eng.tensor_tensor(
                         sall3, sall3,
                         bf16v(o_Cs[dp], 128).unsqueeze(1).broadcast_to([128, IC, TCM]),
                         OP.mult),
                     reads=s_regs + (f"Csb{d % 2}",), writes=s_regs,
                     tag=f"ro{c}_{d}")
                P.op("vector", lambda eng: eng.tensor_tensor(
                         f32v(o_y, 4096), f32v(o_y, 4096), bf16v(o_sall, 2048), OP.add),
                     reads=s_regs + ("yall",), writes=("yall",), tag=f"ya{c}_{d}")

        for j in range(IC):
            ya = f32v(o_y + j * TCM, TCM)
            hbj = bf16v(o_hb, 2048)[:, j * TCM:(j + 1) * TCM]
            yba = bf16v(o_yb, 2048)[:, j * TCM:(j + 1) * TCM]
            P.op("vector", lambda eng, a=ya, b=hbj, j=j: eng.scalar_tensor_tensor(
                     a, b, t_Dv[:, j:j + 1], a, OP.mult, OP.add),
                 reads=(f"hb{j}", "Dv", "yall"), writes=("yall",), tag=f"yD{c}_{j}")
            P.op("scalar", lambda eng, b=ya: eng.activation(t_dt1[:], b, AF.Silu),
                 reads=("yall",), writes=("dt1",), tag=f"ysl{c}_{j}")
            P.op("vector", lambda eng, a=yba: eng.tensor_copy(a, t_dt1[:]),
                 reads=("dt1",), writes=(f"yb{j}",), tag=f"ysc{c}_{j}")

        if debug and c == 0:
            tap("yfin", f32v(o_y, 4096), ("yall",), (128, 4096))
            tap("yb0", bf16v(o_yb, 2048), tuple(f"yb{j}" for j in range(IC)), (128, 4096))
        for h in range(HC):
            bank = 4 + (h % 2)
            for k in range(IC):
                lhs = bf16v(o_wo, 8192)[:, k * 1024 + h * 128: k * 1024 + (h + 1) * 128]
                rhs = bf16v(o_yb, 2048)[:, k * TCM:(k + 1) * TCM]
                MM(ps[bank][:, 0:TCM], lhs, rhs, k == 0, k == IC - 1,
                   ("wo", f"yb{k}"), (f"ps{bank}",), tag=f"op{c}_{h}")
            moa = f32v(o_mo + h * TCM, TCM)
            lna = f32v(o_lnx + h * TCM, TCM)
            P.op("vector", lambda eng, a=moa, b=lna, bk=bank: eng.tensor_tensor(
                     a, ps[bk][:, 0:TCM], b, OP.add),
                 reads=(f"ps{bank}", "lnx"), writes=("mo",), tag=f"ope{c}_{h}")

        if debug and c == 0:
            tap("mo0", f32v(o_mo, 2048), ("mo",), (128, 2048))
        msl = [(f32v(o_mo + h * TCM, TCM), "mo") for h in range(HC)]
        ln_stats(msl, lambda k: (f32v(o_sq + k * TCM, TCM), "sq"),
                 ps_sum=(ps[0], "ps0"), ps_sq=(ps[1], "ps1"),
                 n_feat=D, tc=TCM, regs_src=None, inst=f"lpc{c}")
        ln_rows((ps[0], "ps0"), (ps[1], "ps1"), D, TCM, f"lpc{c}",
                psr=(ps[2], "ps2"), psm=(ps[3], "ps3"))
        for h in range(HC):
            moa = f32v(o_mo + h * TCM, TCM)
            xa = f32v(o_xc + h * TCM, TCM)
            x1a = f32v(o_lnx + h * TCM, TCM)
            P.op("vector", lambda eng, a=x1a, b=moa: eng.tensor_tensor(
                     a, b, ps[2][:, 0:TCM], OP.mult),
                 reads=("mo", "ps2"), writes=("lnx",), tag=f"lpa{c}")
            P.op("vector", lambda eng, a=x1a: eng.tensor_tensor(
                     a, a, ps[3][:, 0:TCM], OP.subtract),
                 reads=("lnx", "ps3"), writes=("lnx",), tag=f"lpb{c}")
            P.op("vector", lambda eng, a=x1a, b=xa: eng.tensor_tensor(
                     a, a, b, OP.add),
                 reads=("lnx", "xc"), writes=("lnx",), tag=f"lpr{c}")
        if debug and c == 0:
            tap("x1c0", f32v(o_lnx, 2048), ("lnx",), (128, 2048))
        xdst = x1h.rearrange("(h p) t -> p h t", p=128)[:, :, cw0:cw0 + TCM]
        P.dma("sync", lambda eng, dp=xdst: eng.dma_start(out=dp, in_=f32v(o_lnx, 2048).rearrange("p (h t) -> p h t", h=8)),
              reads=("lnx",), writes=(f"x1h{c}",), tag=f"stx1{c}")

    P.fence()

    for st in range(NST if do_ffn else 0):
        s0 = st * SUP
        P.op("vector", lambda eng: eng.memset(f32v(o_fac, 4096), 0.0),
             writes=tuple(f"fac{h}" for h in range(4)), tag="fz")
        P.op("vector", lambda eng: eng.memset(f32v(o_fac2, 4096), 0.0),
             writes=tuple(f"fac{h}" for h in range(4, 8)), tag="fz2")

        for ts in range(NTS):
            tc0 = ts * FT
            x1src = x1h.rearrange("(h p) t -> p h t", p=128)[:, :, s0 + tc0:s0 + tc0 + FT]
            P.dma("sync", lambda eng, sp=x1src: eng.dma_start(
                      out=f32v(o_x1t, 4096).rearrange("p (h t) -> p h t", h=8), in_=sp),
                  reads=tuple(f"x1h{c}" for c in range(NCH)),
                  writes=("x1t",), tag=f"ldx1_{st}_{ts}")
            xsl = [(f32v(o_x1t + h * FT, FT), "x1t") for h in range(HC)]
            ln_stats(xsl, lambda k: (f32v(o_scr + k * FT, FT), "scr"),
                     ps_sum=(ps[0], "ps0"), ps_sq=(ps[1], "ps1"),
                     n_feat=D, tc=FT, regs_src=None, inst=f"l1_{st}_{ts}")
            ln_rows((ps[0], "ps0"), (ps[1], "ps1"), D, FT, f"l1_{st}_{ts}",
                    psr=(ps[2], "ps2"), psm=(ps[3], "ps3"))
            for h in range(HC):
                xa = f32v(o_x1t + h * FT, FT)
                fa = f32v(o_scr + h * FT, FT)
                la = bf16v(o_ln1, 4096)[:, h * SUP + tc0: h * SUP + tc0 + FT]
                P.op("vector", lambda eng, a=fa, b=xa: eng.tensor_tensor(
                         a, b, ps[2][:, 0:FT], OP.mult),
                     reads=("x1t", "ps2"), writes=("scr",), tag=f"l1a{st}{ts}")
                P.op("vector", lambda eng, a=fa: eng.tensor_tensor(
                         a, a, ps[3][:, 0:FT], OP.subtract),
                     reads=("scr", "ps3"), writes=("scr",), tag=f"l1b{st}{ts}")
                P.op("vector", lambda eng, a=la, b=fa: eng.tensor_copy(a, b),
                     reads=("scr",), writes=(f"ln1_{h}",), tag=f"l1c{st}{ts}")
            for k in range(HC):
                fa = f32v(o_scr + k * FT, FT)
                MM(ps[4][0:8, 0:FT], t_gw[:, k * 8:(k + 1) * 8], fa,
                   k == 0, k == HC - 1, ("scr", "gw"), ("ps4",), tag=f"gl{st}{ts}")
            P.op("scalar", lambda eng: eng.copy(
                     t_lg[0:8, 0:FT], ps[4][0:8, 0:FT]),
                 reads=("ps4",), writes=("lg",), tag=f"gle{st}{ts}")
            W4 = FT // 128
            for w in range(W4):
                P.op("tensor", lambda eng, w=w: eng.transpose(
                         ps[5][:, w * 8:(w + 1) * 8],
                         t_lg[0:8, w * 128:(w + 1) * 128], t_id[0:8, 0:8]),
                     reads=("lg", "ident"), writes=("ps5",), tag=f"rt1{st}{ts}{w}")
            for w in range(W4):
                P.op("vector", lambda eng, w=w: eng.reduce_max(
                         t_rsc[:, 32 + w:33 + w], ps[5][:, w * 8:(w + 1) * 8], AX.X),
                     reads=("ps5",), writes=("rtop",), tag=f"rt3{st}{ts}{w}")
            P.op("vector", lambda eng: eng.tensor_copy(
                     t_spc[0:1, 0:256], t_spc[0:1, 256:512]),
                 reads=("rtop",), writes=("rtop", "spc"), tag=f"rt3b{st}{ts}")
            P.op("scalar", lambda eng: eng.activation(
                     t_rsc[:, 36:40], t_rsc[:, 32:36], AF.Sigmoid),
                 reads=("rtop",), writes=("rsco",), tag=f"rt4{st}{ts}")
            P.op("scalar", lambda eng: eng.copy(
                     t_spc[0:1, 256:512], t_spc[0:1, 0:256]),
                 reads=("rsco",), writes=("rsco", "spc"), tag=f"rt4b{st}{ts}")
            for w in range(W4):
                P.op("vector", lambda eng, w=w: eng.tensor_scalar(
                         t_rsc[:, 40 + 9 * w:48 + 9 * w], ps[5][:, w * 8:(w + 1) * 8],
                         t_rsc[:, 32 + w:33 + w], None, OP.is_ge),
                     reads=("ps5", "rtop"), writes=("req",), tag=f"rt5{st}{ts}{w}")
            for w in range(W4):
                P.op("vector", lambda eng, w=w: eng.tensor_scalar(
                         t_rsc[:, 40 + 9 * w:48 + 9 * w], t_rsc[:, 40 + 9 * w:48 + 9 * w],
                         t_rsc[:, 36 + w:37 + w], None, OP.mult),
                     reads=("req", "rsco"), writes=("req",), tag=f"rt6{st}{ts}{w}")
            for w in range(W4):
                P.op("vector", lambda eng, w=w: eng.tensor_copy(
                         t_rsc[:, 48 + 9 * w:49 + 9 * w], t_rsc[:, 36 + w:37 + w]),
                     reads=("rsco",), writes=("req",), tag=f"rt7{st}{ts}{w}")
            P.op("vector", lambda eng: eng.tensor_copy(
                     t_spc[0:1, 256:512], t_spc[0:1, 0:256]),
                 reads=("req",), writes=("req", "spc"), tag=f"rt7b{st}{ts}")
            for w in range(W4):
                P.op("tensor", lambda eng, w=w: eng.transpose(
                         ps[6][0:9, w * 128:(w + 1) * 128],
                         t_rsc[:, 40 + 9 * w:49 + 9 * w], t_id[:, 0:128]),
                     reads=("req", "ident"), writes=("ps6",), tag=f"rt8{st}{ts}{w}")
            P.op("vector", lambda eng, tc0=tc0: eng.tensor_copy(
                     t_sm[0:9, tc0:tc0 + FT], ps[6][0:9, 0:FT]),
                 reads=("ps6",), writes=("smr",), tag=f"rt9{st}{ts}")
            if debug and st == 0 and ts == 0:
                tap("rsc0", t_rsc[:, 0:96], ("rtop", "rsco", "req"), (128, 96))
                tap("lg0", t_lg[0:8, 0:FT], ("lg",), (8, FT))

        P.op("vector", lambda eng: eng.tensor_copy(t_smb[0:9, :], t_sm[0:9, :]),
             reads=("smr",), writes=("smb",), tag=f"smcast{st}")

        for ei in range(9):
            routed = ei < 8
            if routed:
                for ts in range(NTS):
                    tc0 = ts * FT
                    MM(ps[7][:, 0:FT], t_sel[0:9, ei * 128:(ei + 1) * 128],
                       t_smb[0:9, tc0:tc0 + FT],
                       True, True, ("smb", "sel"), ("ps7",), tag=f"smb{st}_{ei}")
                    for h in range(HC):
                        la = bf16v(o_ln1, 4096)[:, h * SUP + tc0: h * SUP + tc0 + FT]
                        xa = bf16v(o_xs, 4096)[:, h * SUP + tc0: h * SUP + tc0 + FT]
                        P.op("vector", lambda eng, a=xa, b=la: eng.tensor_tensor(
                                 a, b, ps[7][:, 0:FT], OP.mult),
                             reads=(f"ln1_{h}", "ps7"), writes=(f"xs{h}",),
                             tag=f"xs{st}_{ei}_{h}")
            for ts in range(NTS):
                tc0 = ts * FT
                for q in range(4):
                    sidx = q % 2
                    i0 = q * 512
                    if routed:
                        gsrc, rg = rtg_b[ei, :, i0:i0 + 512], (f"wg{sidx}", f"wcg{ei}")
                        usrc, ru = rtu_b[ei, :, i0:i0 + 512], (f"wu{sidx}", f"wcu{ei}")
                        dsrc, rd2 = rtd_b[ei, i0:i0 + 512, :], (f"wd{sidx}", f"wcd{ei}")
                    else:
                        gsrc, rg = shg_b[:, i0:i0 + 512], (f"wg{sidx}", "wcg8")
                        usrc, ru = shu_b[:, i0:i0 + 512], (f"wu{sidx}", "wcu8")
                        dsrc, rd2 = shd_b[i0:i0 + 512, :], (f"wd{sidx}", "wcd8")
                    P.dma("sync", lambda eng, d_=bf16v(o_wg[sidx], 2048).rearrange("p (c i) -> p c i", c=8),
                          s_=gsrc.rearrange("(c p) i -> p c i", p=128): eng.dma_start(out=d_, in_=s_),
                          reads=rg[1:], writes=rg[:1], tag=f"lwg{st}_{ei}_{ts}_{q}")
                    P.dma("sync", lambda eng, d_=bf16v(o_wu[sidx], 2048).rearrange("p (c i) -> p c i", c=8),
                          s_=usrc.rearrange("(c p) i -> p c i", p=128): eng.dma_start(out=d_, in_=s_),
                          reads=ru[1:], writes=ru[:1], tag=f"lwu{st}_{ei}_{ts}_{q}")
                    P.dma("sync", lambda eng, d_=bf16v(o_wd[sidx], 2048).rearrange("p (c h) -> p c h", c=4),
                          s_=dsrc.rearrange("(c p) h -> p c h", p=128): eng.dma_start(out=d_, in_=s_),
                          reads=rd2[1:], writes=rd2[:1], tag=f"lwd{st}_{ei}_{ts}_{q}")
                    for il in range(4):
                        bg = (il % 2) * 2
                        bu = bg + 1
                        for k in range(HC):
                            lhs = bf16v(o_wg[sidx], 2048)[:, k * 512 + il * 128:
                                                          k * 512 + (il + 1) * 128]
                            if routed:
                                rhs = bf16v(o_xs, 4096)[:, k * SUP + tc0:
                                                        k * SUP + tc0 + FT]
                                rr = f"xs{k}"
                            else:
                                rhs = bf16v(o_ln1, 4096)[:, k * SUP + tc0:
                                                         k * SUP + tc0 + FT]
                                rr = f"ln1_{k}"
                            MM(ps[bg][:, 0:FT], lhs, rhs, k == 0, k == HC - 1,
                               (f"wg{sidx}", rr), (f"ps{bg}",), tag=f"g{ei}{ts}{q}{il}")
                        for k in range(HC):
                            lhs = bf16v(o_wu[sidx], 2048)[:, k * 512 + il * 128:
                                                          k * 512 + (il + 1) * 128]
                            if routed:
                                rhs = bf16v(o_xs, 4096)[:, k * SUP + tc0:
                                                        k * SUP + tc0 + FT]
                                rr = f"xs{k}"
                            else:
                                rhs = bf16v(o_ln1, 4096)[:, k * SUP + tc0:
                                                         k * SUP + tc0 + FT]
                                rr = f"ln1_{k}"
                            MM(ps[bu][:, 0:FT], lhs, rhs, k == 0, k == HC - 1,
                               (f"wu{sidx}", rr), (f"ps{bu}",), tag=f"u{ei}{ts}{q}{il}")
                        tpar = il % 2
                        tsa = t_tss[:, tpar * 512:tpar * 512 + FT]
                        P.op("scalar", lambda eng, a=tsa, bg=bg: eng.activation(
                                 a, ps[bg][:, 0:FT], AF.Silu),
                             reads=(f"ps{bg}",), writes=(f"tss{tpar}",),
                             tag=f"sg{ei}{ts}{q}{il}")
                        aa = bf16v(o_act[sidx], 1024)[:, il * FT:(il + 1) * FT]
                        P.op("vector", lambda eng, a=aa, b=tsa, bu=bu:
                                 eng.tensor_tensor(a, b, ps[bu][:, 0:FT], OP.mult),
                             reads=(f"tss{tpar}", f"ps{bu}"), writes=(f"act{sidx}_{il}",),
                             tag=f"am{ei}{ts}{q}{il}")
                    for hg in range(2):
                        for hh in range(4):
                            h = hg * 4 + hh
                            bank = 4 + hh
                            for il in range(4):
                                lhs = bf16v(o_wd[sidx], 2048)[:, il * 1024 + h * 128:
                                                              il * 1024 + (h + 1) * 128]
                                rhs = bf16v(o_act[sidx], 1024)[:, il * FT:(il + 1) * FT]
                                MM(ps[bank][:, 0:FT], lhs, rhs, il == 0, il == 3,
                                   (f"wd{sidx}", f"act{sidx}_{il}"), (f"ps{bank}",),
                                   tag=f"dn{ei}{ts}{q}{h}")
                        for hh in range(4):
                            h = hg * 4 + hh
                            bank = 4 + hh
                            fa = fac_slice(h, tc0, FT)
                            P.op("vector", lambda eng, a=fa, bk=bank:
                                     eng.tensor_tensor(a, a, ps[bk][:, 0:FT], OP.add),
                                 reads=(f"ps{bank}", f"fac{h}"), writes=(f"fac{h}",),
                                 tag=f"fa{ei}{ts}{q}{h}")

        for ts in range(NTS):
            tc0 = ts * FT
            x1src = x1h.rearrange("(h p) t -> p h t", p=128)[:, :, s0 + tc0:s0 + tc0 + FT]
            P.dma("sync", lambda eng, sp=x1src: eng.dma_start(
                      out=f32v(o_x1t, 4096).rearrange("p (h t) -> p h t", h=8), in_=sp),
                  reads=tuple(f"x1h{c}" for c in range(NCH)),
                  writes=("x1t",), tag=f"ldx1b_{st}_{ts}")
            fsl = [(fac_slice(h, tc0, FT), f"fac{h}") for h in range(HC)]
            ln_stats(fsl, lambda k: (f32v(o_scr + k * FT, FT), "scr"),
                     ps_sum=(ps[0], "ps0"), ps_sq=(ps[1], "ps1"),
                     n_feat=D, tc=FT, regs_src=None, inst=f"l2_{st}_{ts}")
            ln_rows((ps[0], "ps0"), (ps[1], "ps1"), D, FT, f"l2_{st}_{ts}",
                    psr=(ps[2], "ps2"), psm=(ps[3], "ps3"))
            for h in range(HC):
                fa = fac_slice(h, tc0, FT)
                xa = f32v(o_x1t + h * FT, FT)
                oa = f32v(o_out + h * FT, FT)
                P.op("vector", lambda eng, a=oa, b=fa: eng.tensor_tensor(
                         a, b, ps[2][:, 0:FT], OP.mult),
                     reads=(f"fac{h}", "ps2"), writes=("outb",), tag=f"l2a{st}{ts}")
                P.op("vector", lambda eng, a=oa: eng.tensor_tensor(
                         a, a, ps[3][:, 0:FT], OP.subtract),
                     reads=("outb", "ps3"), writes=("outb",), tag=f"l2b{st}{ts}")
                P.op("vector", lambda eng, a=oa, b=xa: eng.tensor_tensor(
                         a, a, b, OP.add),
                     reads=("outb", "x1t"), writes=("outb",), tag=f"l2r{st}{ts}")
            odst = outT.rearrange("(h p) t -> p h t", p=128)[:, :, s0 + tc0:s0 + tc0 + FT]
            P.dma("sync", lambda eng, dp=odst: eng.dma_start(
                      out=dp, in_=f32v(o_out, 4096).rearrange("p (h t) -> p h t", h=8)),
                  reads=("outb",), writes=(f"outh{st}{ts}",), tag=f"sto{st}{ts}")

    P.analyze()
    bodyf = P.emit(nc, sems)

    with nc.Block() as block:
        @block.sync
        def _(eng):
            bodyf(eng, "sync")
            eng.wait_ge(sems["qsync"], 16 * P.final_qcnt["qsync"])
            eng.wait_ge(sems["qgps"], 16 * P.final_qcnt["qgps"])

        @block.gpsimd
        def _(eng):
            bodyf(eng, "gpsimd")

        @block.vector
        def _(eng):
            bodyf(eng, "vector")

        @block.scalar
        def _(eng):
            bodyf(eng, "scalar")

        @block.tensor
        def _(eng):
            bodyf(eng, "tensor")

    ctx.close()
    return nc


_NC = None
DEBUG = False


def _host_inputs(inputs):
    g = {}
    for k, v in inputs.items():
        a = np.asarray(v)
        if a.dtype != np.float32:
            a = a.astype(np.float32)
        g[k] = a
    x = g["L0_attn_x"]
    base = {}
    base["wiT"] = np.ascontiguousarray(g["L0_attn_in_proj_w"].T)
    base["woT"] = np.ascontiguousarray(g["L0_attn_out_proj_w"].T)
    base["xpT"] = np.ascontiguousarray(g["L0_attn_x_proj_w"].T)
    base["Als"] = np.ascontiguousarray(
        g["L0_attn_A_log"].reshape(16, 128, 16).transpose(1, 0, 2).reshape(128, 256))
    base["wdt"] = np.ascontiguousarray(g["L0_attn_dt_proj_w"].reshape(16, 128).T)
    base["bdt"] = np.ascontiguousarray(g["L0_attn_dt_proj_b"].reshape(16, 128).T)
    base["cw"] = np.ascontiguousarray(
        g["L0_attn_conv_w"].reshape(16, 128, 4).transpose(1, 0, 2).reshape(128, 64))
    base["Dv"] = np.ascontiguousarray(g["L0_attn_D"].reshape(16, 128).T)
    base["nw"] = np.ascontiguousarray(g["L0_attn_norm_w"].reshape(8, 128).T)
    base["nb"] = np.ascontiguousarray(g["L0_attn_norm_b"].reshape(8, 128).T)
    base["gw"] = np.ascontiguousarray(g["L0_ffn_gate_w"])
    base["ones34"] = np.ones((34, 128), np.float32)
    sel = np.zeros((33, 33, 128), np.float32)
    for jj in range(33):
        sel[jj, jj, :] = 1.0
    base["selmat"] = np.ascontiguousarray(sel.transpose(1, 0, 2).reshape(33, 33 * 128))
    base["ident"] = np.eye(128, dtype=np.float32)
    base["onescol"] = np.ones((128, 1), np.float32)
    base["shg"] = np.ascontiguousarray(g["L0_ffn_sh_gate"])
    base["shu"] = np.ascontiguousarray(g["L0_ffn_sh_up"])
    base["shd"] = np.ascontiguousarray(g["L0_ffn_sh_down"])
    base["rtg"] = np.ascontiguousarray(g["L0_ffn_rt_gate"])
    base["rtu"] = np.ascontiguousarray(g["L0_ffn_rt_up"])
    base["rtd"] = np.ascontiguousarray(g["L0_ffn_rt_down"])
    in_maps = []
    for c in range(8):
        m = dict(base)
        m["xT"] = np.ascontiguousarray(x[c].T)
        in_maps.append(m)
    return in_maps


def kernel(**inputs):
    global _NC
    if _NC is None:
        _NC = build_nc()
    in_maps = _host_inputs(inputs)
    res = run_bass_kernel_spmd(_NC, in_maps, core_ids=list(range(8)))
    out = np.stack([np.asarray(res.results[c]["outT"]).T for c in range(8)], axis=0)
    return np.ascontiguousarray(out.astype(np.float32))


# version 09 (final)
# speedup vs baseline: 1.2083x; 1.0425x over previous
import numpy as np
import concourse.bass as bass
import concourse.mybir as mybir
from concourse.bass_utils import run_bass_kernel_spmd

F32 = mybir.dt.float32
BF16 = mybir.dt.bfloat16
AF = mybir.ActivationFunctionType
OP = mybir.AluOpType
AX = mybir.AxisListType

D = 1024
DI = 2048
T = 2048
DS = 16
HC = 8
IC = 16
TCM = 256
NCH = T // TCM
SUP = 1024
FT = 512
NST = T // SUP
NTS = SUP // FT
EPS = 1e-5

ENGINES = ("tensor", "vector", "scalar", "gpsimd", "sync")
QUEUES = ("qsync", "qgps")


class Plan:

    def __init__(self):
        self.ops = []
        self.fences = []

    def op(self, stream, fn, reads=(), writes=(), queue=None, tag=""):
        assert stream in ENGINES
        self.ops.append(dict(
            stream=stream, fn=fn, reads=tuple(reads), writes=tuple(writes),
            queue=queue, tag=tag, deps=[], signal=False, sig=None, waits=None,
        ))

    def dma(self, stream, fn, reads=(), writes=(), tag=""):
        q = "qgps" if stream == "gpsimd" else "qsync"
        self.op(stream, fn, reads, writes, queue=q, tag=tag)

    def fence(self):
        self.fences.append(len(self.ops))

    PARANOID = False

    def analyze(self):
        last_w = {}
        readers = {}
        fence_set = set(self.fences)
        last_of = {}
        fence_deps = None
        fence_waiting = set()
        for i, o in enumerate(self.ops):
            if i in fence_set:
                fence_deps = dict(last_of)
                fence_waiting = set(ENGINES)
            deps = set()
            if fence_deps is not None and o["stream"] in fence_waiting:
                for s, j in fence_deps.items():
                    deps.add(j)
                fence_waiting.discard(o["stream"])
                if not fence_waiting:
                    fence_deps = None
            for r in o["reads"]:
                if r in last_w:
                    deps.add(last_w[r])
            for r in o["writes"]:
                if r in last_w:
                    deps.add(last_w[r])
                for j in readers.get(r, ()):
                    deps.add(j)
            own = o["queue"] or o["stream"]
            deps.discard(i)
            dd = []
            for j in deps:
                p = self.ops[j]
                psrc = p["queue"] or p["stream"]
                if psrc == own and p["queue"] is None and o["queue"] is None:
                    continue
                if p["queue"] is not None and p["queue"] == o.get("queue") \
                        and p["stream"] == o["stream"]:
                    continue
                if p["queue"] is None and p["stream"] == o["stream"]:
                    continue
                dd.append(j)
            o["deps"] = dd
            for j in dd:
                self.ops[j]["signal"] = True
            for r in o["reads"]:
                readers.setdefault(r, []).append(i)
            for r in o["writes"]:
                last_w[r] = i
                readers[r] = []
            last_of[own] = i
        cnt = {s: 0 for s in ENGINES}
        qcnt = {q: 0 for q in QUEUES}
        for o in self.ops:
            if o["queue"] is not None:
                qcnt[o["queue"]] += 1
                o["sig"] = qcnt[o["queue"]]
                o["signal"] = True
            elif o["signal"]:
                cnt[o["stream"]] += 1
                o["sig"] = cnt[o["stream"]]
        self.final_qcnt = qcnt
        clocks = {s: {} for s in ENGINES}

        def merge(c, snap):
            for k, v in snap.items():
                if c.get(k, 0) < v:
                    c[k] = v

        for o in self.ops:
            s = o["stream"]
            c = clocks[s]
            waits = {}
            for j in o["deps"]:
                p = self.ops[j]
                src = p["queue"] or p["stream"]
                v = p["sig"]
                assert v is not None
                if not Plan.PARANOID and c.get(src, 0) >= v:
                    continue
                if waits.get(src, 0) < v:
                    waits[src] = v
            for src, v in waits.items():
                pass
            for j in o["deps"]:
                p = self.ops[j]
                src = p["queue"] or p["stream"]
                if src in waits and p["sig"] <= waits[src]:
                    merge(c, p.get("snap", {}))
            for src, v in waits.items():
                if c.get(src, 0) < v:
                    c[src] = v
            o["waits"] = sorted(waits.items())
            if o["signal"]:
                src = o["queue"] or s
                snap = dict(c)
                snap[src] = o["sig"]
                if o["queue"] is None:
                    snap[s] = o["sig"]
                o["snap"] = snap

    def emit(self, nc, sems):
        engine_ops = {s: [] for s in ENGINES}
        for o in self.ops:
            engine_ops[o["stream"]].append(o)

        def body(eng, s):
            for o in engine_ops[s]:
                for src, v in o["waits"]:
                    val = v * 16 if src in QUEUES else v
                    eng.wait_ge(sems[src], val)
                inst = o["fn"](eng)
                if o["queue"] is not None:
                    inst.then_inc(sems[o["queue"]], 16)
                elif o["signal"]:
                    inst.then_inc(sems[o["stream"]], 1)
        return body


def ceil_div(a, b):
    return -(-a // b)


kernel_scan_stub = False


def build_nc(debug=False, nch=NCH, do_ffn=True):
    nc = bass.Bass("TRN2", target_bir_lowering=False)
    dbg_outs = {}

    dr = {}
    def din(name, shape):
        dr[name] = nc.dram_tensor(name, list(shape), F32, kind="ExternalInput").ap()
    din("xT", (D, T))
    din("wiT", (D, DI))
    din("woT", (DI, D))
    din("xpT", (DI, 33))
    din("Als", (128, 256))
    din("wdt", (128, 16))
    din("bdt", (128, 16))
    din("cw", (128, 64))
    din("Dv", (128, 16))
    din("nw", (128, 8))
    din("nb", (128, 8))
    din("gw", (D, 8))
    din("ones34", (34, 128))
    din("selmat", (33, 33 * 128))
    din("ident", (128, 128))
    din("onescol", (128, 1))
    din("shg", (D, DI))
    din("shu", (D, DI))
    din("shd", (DI, D))
    din("rtg", (8, D, DI))
    din("rtu", (8, D, DI))
    din("rtd", (8, DI, D))
    x1h = nc.dram_tensor("x1h", [D, T], F32, kind="Internal").ap()
    rtg_b = nc.dram_tensor("rtg_b", [8, D, DI], BF16, kind="Internal").ap()
    rtu_b = nc.dram_tensor("rtu_b", [8, D, DI], BF16, kind="Internal").ap()
    rtd_b = nc.dram_tensor("rtd_b", [8, DI, D], BF16, kind="Internal").ap()
    shg_b = nc.dram_tensor("shg_b", [D, DI], BF16, kind="Internal").ap()
    shu_b = nc.dram_tensor("shu_b", [D, DI], BF16, kind="Internal").ap()
    shd_b = nc.dram_tensor("shd_b", [DI, D], BF16, kind="Internal").ap()
    outT = nc.dram_tensor("outT", [D, T], F32, kind="ExternalOutput").ap()

    import contextlib
    ctx = contextlib.ExitStack()
    e = ctx.enter_context
    t_Als = e(nc.sbuf_tensor("t_Als", [128, 256], F32))
    t_wdt = e(nc.sbuf_tensor("t_wdt", [128, 16], F32))
    t_bdt = e(nc.sbuf_tensor("t_bdt", [128, 16], F32))
    t_cw = e(nc.sbuf_tensor("t_cw", [128, 64], F32))
    t_Dv = e(nc.sbuf_tensor("t_Dv", [128, 16], F32))
    t_nw = e(nc.sbuf_tensor("t_nw", [128, 8], F32))
    t_nb = e(nc.sbuf_tensor("t_nb", [128, 8], F32))
    t_gw = e(nc.sbuf_tensor("t_gw", [128, 64], F32))
    t_ones = e(nc.sbuf_tensor("t_ones", [34, 128], F32))
    t_sel = e(nc.sbuf_tensor("t_sel", [33, 33 * 128], BF16))
    t_id = e(nc.sbuf_tensor("t_id", [128, 128], F32))
    t_oc = e(nc.sbuf_tensor("t_oc", [128, 1], F32))
    t_states = e(nc.sbuf_tensor("t_states", [128, 256], F32))
    t_srow = e(nc.sbuf_tensor("t_srow", [1, 1536], F32))
    t_sm = e(nc.sbuf_tensor("t_sm", [16, SUP], BF16))
    t_lg = e(nc.sbuf_tensor("t_lg", [16, FT], F32))
    t_rsc = e(nc.sbuf_tensor("t_rsc", [128, 84], F32))
    t_spc = e(nc.sbuf_tensor("t_spc", [1, 512], F32))
    t_projc = e(nc.sbuf_tensor("t_projc", [33, 256], F32))
    t_projb = e(nc.sbuf_tensor("t_projb", [33, 256], BF16))
    t_dt1 = e(nc.sbuf_tensor("t_dt1", [128, 256], F32))
    t_tss = e(nc.sbuf_tensor("t_tss", [128, 1024], BF16))

    BIGW = 45600
    big = e(nc.sbuf_tensor("big", [128, BIGW], F32))

    ps = [e(nc.psum_tensor(f"ps{i}", [128, 512], F32)) for i in range(8)]

    sems = {}
    for s in ENGINES + QUEUES:
        sems[s] = e(nc.semaphore(f"sem_{s}"))

    off = [0]
    def alloc(w):
        o = off[0]
        off[0] += w
        assert off[0] <= BIGW, f"big overflow {off[0]}"
        return o
    viewlog = []
    def f32v(o, w):
        viewlog.append((len(P.ops) if 'P' in dir() else -1, o, w, 'f32'))
        return big[:, o:o + w]
    def bf16v(o, w32):
        viewlog.append((len(P.ops) if 'P' in dir() else -1, o, w32, 'bf16'))
        return big[:, o:o + w32].bitcast(BF16)
    nc._viewlog = viewlog

    o_wi = alloc(8192)
    o_wo = alloc(8192)
    o_xp = alloc(272)
    o_xc = alloc(2048)
    o_lnx = alloc(2048)
    o_h0b = alloc(1024)
    o_hin = alloc(16 * 259)
    o_hb = alloc(2048)
    o_dt = alloc(2048)
    o_y = alloc(4096)
    o_yb = alloc(2048)
    o_mo = alloc(2048)
    o_sq = alloc(2048)
    o_sc = alloc(2 * 256)
    o_sall = alloc(2048)
    o_bxa = alloc(2048)
    o_Bs = [alloc(128), alloc(128)]
    o_Cs = [alloc(128), alloc(128)]
    o_cacc = o_sq
    m_end = off[0]
    assert m_end <= BIGW

    off[0] = 0
    o_x1t = alloc(4096)
    o_fac = alloc(4096)
    o_fac2 = alloc(4096)
    o_ln1 = alloc(4096)
    o_scr = alloc(4096)
    o_wg = [alloc(2048), alloc(2048)]
    o_wu = [alloc(2048), alloc(2048)]
    o_wd = [alloc(2048), alloc(2048)]
    o_xs = alloc(4096)
    o_act = [alloc(1024), alloc(1024)]
    o_out = alloc(4096)
    f_end = off[0]
    assert f_end <= BIGW

    def fac_slice(h, c0, w):
        base = h * SUP + c0
        assert base // 4096 == (base + w - 1) // 4096
        o = o_fac if base < 4096 else o_fac2
        return f32v(o + base % 4096, w)

    P = Plan()

    def ld(dst_ap, src_ap, regs, cast=False, tag=""):
        s = "gpsimd" if cast else "sync"
        P.dma(s, lambda eng, d=dst_ap, sp=src_ap: eng.dma_start(out=d, in_=sp),
              reads=(), writes=regs, tag=tag)

    ld(t_Als[:], dr["Als"], ("Als",))
    ld(t_wdt[:], dr["wdt"], ("wdt",))
    ld(t_bdt[:], dr["bdt"], ("bdt",))
    ld(t_cw[:], dr["cw"], ("cw",))
    ld(t_Dv[:], dr["Dv"], ("Dv",))
    ld(t_nw[:], dr["nw"], ("nw",))
    ld(t_nb[:], dr["nb"], ("nb",))
    ld(t_gw[:].rearrange("p (c e) -> p c e", c=8), dr["gw"].rearrange("(c p) e -> p c e", p=128), ("gw",))
    ld(t_ones[:], dr["ones34"], ("ones",))
    ld(t_sel[:], dr["selmat"], ("sel",), cast=True)
    ld(t_id[:], dr["ident"], ("ident",))
    ld(t_oc[:], dr["onescol"], ("onescol",))
    ld(bf16v(o_wi, 8192).rearrange("p (c i) -> p c i", c=8), dr["wiT"].rearrange("(c p) i -> p c i", p=128),
       ("wi",), cast=True)
    ld(bf16v(o_wo, 8192).rearrange("p (c h) -> p c h", c=16), dr["woT"].rearrange("(c p) h -> p c h", p=128),
       ("wo",), cast=True)
    ld(bf16v(o_xp, 264)[:, 0:528].rearrange("p (c m) -> p c m", c=16), dr["xpT"].rearrange("(c p) m -> p c m", p=128),
       ("xp",), cast=True)

    for ee in range(8):
        ld(rtg_b[ee], dr["rtg"][ee], (f"wcg{ee}",), cast=True, tag=f"pcg{ee}")
        ld(rtu_b[ee], dr["rtu"][ee], (f"wcu{ee}",), cast=True, tag=f"pcu{ee}")
        ld(rtd_b[ee], dr["rtd"][ee], (f"wcd{ee}",), cast=True, tag=f"pcd{ee}")
    ld(shg_b, dr["shg"], ("wcg8",), cast=True, tag="pcg8")
    ld(shu_b, dr["shu"], ("wcu8",), cast=True, tag="pcu8")
    ld(shd_b, dr["shd"], ("wcd8",), cast=True, tag="pcd8")

    P.op("scalar", lambda eng: eng.activation(t_Als[:], t_Als[:], AF.Exp),
         reads=("Als",), writes=("Als",), tag="Aexp")
    P.op("vector", lambda eng: eng.tensor_scalar_mul(t_Als[:], t_Als[:], -1.0),
         reads=("Als",), writes=("Als",), tag="Aneg")
    P.op("vector", lambda eng: eng.memset(t_states[:], 0.0),
         writes=("states",), tag="st0")
    P.op("vector", lambda eng: eng.memset(t_spc[:], 0.0), writes=("spc",), tag="spc0")
    for j in range(IC):
        o = o_hin + j * 259
        P.op("vector", lambda eng, o=o: eng.memset(f32v(o, 3), 0.0),
             writes=(f"hin{j}",), tag="halo0")

    def tap(name, ap, regions, shape2):
        if not debug:
            return
        if ap.dtype != F32 and debug != "all":
            return
        o = nc.dram_tensor("dbg_" + name, list(shape2), F32, kind="ExternalOutput").ap()
        dbg_outs[name] = o
        cast = ap.dtype != F32
        s = "gpsimd" if cast else "sync"
        P.dma(s, lambda eng, a=ap, oo=o: eng.dma_start(out=oo, in_=a),
              reads=tuple(regions), writes=("dbg_" + name,), tag="tap_" + name)

    def MM(psl, lhsT, rhs, start, stop, reads, wr, tag=""):
        P.op("tensor",
             lambda eng, a=psl, b=lhsT, c=rhs, s1=start, s2=stop:
                 eng.matmul(a, b, c, start=s1, stop=s2),
             reads=reads, writes=wr, tag=tag)

    def ln_stats(src_slices, sq_dst, ps_sum, ps_sq, n_feat, tc, regs_src, inst, do_sq=True):
        nk = len(src_slices)
        for k, (apx, rg) in enumerate(src_slices):
            a2, rg2 = sq_dst(k)
            P.op("scalar", lambda eng, o=a2, i=apx: eng.activation(o, i, AF.Square),
                 reads=(rg,), writes=(rg2,), tag=f"sq{inst}")
        for k, (apx, rg) in enumerate(src_slices):
            MM(ps_sum[0][0:1, 0:tc], t_oc[:], apx, k == 0, k == nk - 1,
               (rg, "onescol"), (ps_sum[1],), tag=f"sum{inst}")
        for k in range(nk):
            a2, rg2 = sq_dst(k)
            MM(ps_sq[0][0:1, 0:tc], t_oc[:], a2, k == 0, k == nk - 1,
               (rg2, "onescol"), (ps_sq[1],), tag=f"ssq{inst}")

    rowc = [0]
    def ln_rows(ps_sum, ps_sq, n_feat, tc, inst, psr, psm):
        c0 = 0
        mu = t_srow[0:1, c0:c0 + tc]
        m2 = t_srow[0:1, 512:512 + tc]
        rs = t_srow[0:1, 1024:1024 + tc]
        rmu = f"r_mu{inst}"; rm2 = f"r_m2{inst}"; rrs = f"r_rs{inst}"
        P.op("vector", lambda eng: eng.tensor_scalar_mul(mu, ps_sum[0][0:1, 0:tc], 1.0 / n_feat),
             reads=(ps_sum[1],), writes=(rmu,), tag=f"mu{inst}")
        P.op("vector", lambda eng: eng.tensor_scalar_mul(m2, ps_sq[0][0:1, 0:tc], 1.0 / n_feat),
             reads=(ps_sq[1],), writes=(rm2,), tag=f"m2{inst}")
        P.op("vector", lambda eng: eng.tensor_tensor(rs, mu, mu, OP.mult),
             reads=(rmu,), writes=(rrs,), tag=f"musq{inst}")
        P.op("vector", lambda eng: eng.tensor_tensor(m2, m2, rs, OP.subtract),
             reads=(rm2, rrs), writes=(rm2,), tag=f"var{inst}")
        P.op("vector", lambda eng: eng.tensor_scalar_add(m2, m2, EPS),
             reads=(rm2,), writes=(rm2,), tag=f"veps{inst}")
        P.op("scalar", lambda eng: eng.activation(m2, m2, AF.Ln),
             reads=(rm2,), writes=(rm2,), tag=f"lnv{inst}")
        P.op("vector", lambda eng: eng.tensor_scalar_mul(m2, m2, -0.5),
             reads=(rm2,), writes=(rm2,), tag=f"vneg{inst}")
        P.op("scalar", lambda eng: eng.activation(rs, m2, AF.Exp),
             reads=(rm2,), writes=(rrs,), tag=f"rstd{inst}")
        P.op("vector", lambda eng: eng.tensor_tensor(mu, mu, rs, OP.mult),
             reads=(rmu, rrs), writes=(rmu,), tag=f"mup{inst}")
        MM(psr[0][:, 0:tc], t_ones[0:1, :], rs, True, True,
           (rrs, "ones"), (psr[1],), tag=f"bcr{inst}")
        MM(psm[0][:, 0:tc], t_ones[0:1, :], mu, True, True,
           (rmu, "ones"), (psm[1],), tag=f"bcm{inst}")

    for c in range(nch):
        cw0 = c * TCM
        xsrc = dr["xT"].rearrange("(h p) t -> p h t", p=128)[:, :, cw0:cw0 + TCM]
        P.dma("sync", lambda eng, sp=xsrc: eng.dma_start(out=f32v(o_xc, 2048).rearrange("p (h t) -> p h t", h=8), in_=sp),
              writes=("xc",), tag=f"ldx{c}")

        xsl = [(f32v(o_xc + h * TCM, TCM), "xc") for h in range(HC)]
        ln_stats(xsl, lambda k: (f32v(o_sq + k * TCM, TCM), "sq"),
                 ps_sum=(ps[0], "ps0"), ps_sq=(ps[1], "ps1"),
                 n_feat=D, tc=TCM, regs_src=None, inst=f"l0c{c}")
        ln_rows((ps[0], "ps0"), (ps[1], "ps1"), D, TCM, f"l0c{c}",
                psr=(ps[2], "ps2"), psm=(ps[3], "ps3"))
        for h in range(HC):
            xa = f32v(o_xc + h * TCM, TCM)
            la = f32v(o_lnx + h * TCM, TCM)
            P.op("vector", lambda eng, a=la, b=xa: eng.tensor_tensor(
                     a, b, ps[2][:, 0:TCM], OP.mult),
                 reads=("xc", "ps2"), writes=("lnx",), tag=f"l0a{c}")
            P.op("vector", lambda eng, a=la: eng.tensor_tensor(
                     a, a, ps[3][:, 0:TCM], OP.subtract),
                 reads=("lnx", "ps3"), writes=("lnx",), tag=f"l0b{c}")

        if debug and c == 0:
            tap("lnx0", f32v(o_lnx, 2048), ("lnx",), (128, 2048))
        lsl = [(f32v(o_lnx + h * TCM, TCM), "lnx") for h in range(HC)]
        ln_stats(lsl, lambda k: (f32v(o_sq + k * TCM, TCM), "sq"),
                 ps_sum=(ps[0], "ps0"), ps_sq=(ps[1], "ps1"),
                 n_feat=D, tc=TCM, regs_src=None, inst=f"lic{c}")
        ln_rows((ps[0], "ps0"), (ps[1], "ps1"), D, TCM, f"lic{c}",
                psr=(ps[2], "ps2"), psm=(ps[3], "ps3"))
        for h in range(HC):
            la = f32v(o_lnx + h * TCM, TCM)
            sqa = f32v(o_sq + h * TCM, TCM)
            ha = bf16v(o_h0b, 1024)[:, h * TCM:(h + 1) * TCM]
            P.op("vector", lambda eng, a=sqa, b=la: eng.tensor_tensor(
                     a, b, ps[2][:, 0:TCM], OP.mult),
                 reads=("lnx", "ps2"), writes=("sq",), tag=f"lia{c}")
            P.op("vector", lambda eng, a=sqa: eng.tensor_tensor(
                     a, a, ps[3][:, 0:TCM], OP.subtract),
                 reads=("sq", "ps3"), writes=("sq",), tag=f"lib{c}")
            P.op("vector", lambda eng, a=ha, b=sqa, h=h: eng.tensor_scalar(
                     a, b, t_nw[:, h:h + 1], t_nb[:, h:h + 1], OP.mult, OP.add),
                 reads=("sq", "nw", "nb"), writes=("h0b",), tag=f"lic{c}")

        if debug and c == 0:
            tap("h0b0", bf16v(o_h0b, 1024), ("h0b",), (128, 2048))
        for j in range(IC):
            bank = 4 + (j % 2)
            for k in range(HC):
                lhs = bf16v(o_wi, 8192)[:, k * 2048 + j * 128: k * 2048 + (j + 1) * 128]
                rhs = bf16v(o_h0b, 1024)[:, k * TCM:(k + 1) * TCM]
                MM(ps[bank][:, 0:TCM], lhs, rhs, k == 0, k == HC - 1,
                   ("wi", "h0b"), (f"ps{bank}",), tag=f"ip{c}_{j}")
            o = o_hin + j * 259
            if c > 0:
                P.op("vector", lambda eng, o=o: eng.tensor_copy(
                         f32v(o, 3), f32v(o + 256, 3)),
                     reads=(f"hin{j}",), writes=(f"hin{j}",), tag=f"halo{c}_{j}")
            P.op("scalar", lambda eng, o=o, b=bank: eng.copy(
                     f32v(o + 3, 256), ps[b][:, 0:TCM]),
                 reads=(f"ps{bank}",), writes=(f"hin{j}",), tag=f"ipe{c}_{j}")

        for j in range(IC):
            o = o_hin + j * 259
            acc = f32v(o_cacc + (j % 8) * TCM, TCM)
            racc = f"cacc{j % 8}"
            P.op("vector", lambda eng, a=acc, o=o, j=j: eng.tensor_scalar(
                     a, f32v(o + 0, 256), t_cw[:, j * 4:j * 4 + 1], None, OP.mult),
                 reads=(f"hin{j}", "cw"), writes=(racc,), tag=f"cv0{c}_{j}")
            for k in (1, 2, 3):
                P.op("vector", lambda eng, a=acc, o=o, j=j, k=k:
                         eng.scalar_tensor_tensor(
                             a, f32v(o + k, 256), t_cw[:, j * 4 + k:j * 4 + k + 1],
                             a, OP.mult, OP.add),
                     reads=(f"hin{j}", "cw", racc), writes=(racc,), tag=f"cv{k}{c}_{j}")
            hba = bf16v(o_hb, 2048)[:, j * TCM:(j + 1) * TCM]
            P.op("scalar", lambda eng, a=hba, b=acc: eng.activation(a, b, AF.Silu),
                 reads=(racc,), writes=(f"hb{j}",), tag=f"silu{c}_{j}")

        if debug and c == 0:
            tap("hin0", f32v(o_hin, 16 * 259), tuple(f"hin{j}" for j in range(IC)), (128, 16 * 259))
            tap("hb0", bf16v(o_hb, 2048), tuple(f"hb{j}" for j in range(IC)), (128, 4096))
        for k in range(IC):
            lhs = bf16v(o_xp, 264)[:, k * 33:(k + 1) * 33]
            rhs = bf16v(o_hb, 2048)[:, k * TCM:(k + 1) * TCM]
            MM(ps[6][0:33, 0:TCM], lhs, rhs, k == 0, k == IC - 1,
               ("xp", f"hb{k}"), ("ps6",), tag=f"xp{c}")
        P.op("scalar", lambda eng: eng.copy(t_projc[:], ps[6][0:33, 0:TCM]),
             reads=("ps6",), writes=("projc",), tag=f"xpe{c}")
        P.op("vector", lambda eng: eng.tensor_copy(t_projb[:], t_projc[:]),
             reads=("projc",), writes=("projb",), tag=f"xpb{c}")

        if debug and c == 0:
            tap("projc0", t_projc[:], ("projc",), (33, 256))
        MM(ps[7][:, 0:TCM], t_sel[0:33, 0:128], t_projb[0:33, :], True, True,
           ("projb", "sel"), ("ps7",), tag=f"dti{c}")
        for j in range(IC):
            P.op("scalar", lambda eng, j=j: eng.activation(
                     t_dt1[:], ps[7][:, 0:TCM], AF.Exp,
                     bias=t_bdt[:, j:j + 1], scale=t_wdt[:, j:j + 1]),
                 reads=("ps7", "wdt", "bdt"), writes=("dt1",), tag=f"dte{c}_{j}")
            P.op("scalar", lambda eng, j=j: eng.activation(
                     bf16v(o_dt, 2048)[:, j * TCM:(j + 1) * TCM], t_dt1[:], AF.Ln, bias=1.0),
                 reads=("dt1",), writes=(f"dt{j}",), tag=f"dtl{c}_{j}")

        if debug and c == 0:
            tap("dt0", f32v(o_dt, 4096), tuple(f"dt{j}" for j in range(IC)), (128, 4096))
        for d in range(DS):
            bB = 4 + (d % 2) * 2
            bC = bB + 1
            MM(ps[bB][:, 0:TCM], t_sel[0:33, (1 + d) * 128:(2 + d) * 128],
               t_projb[0:33, :], True, True, ("projb", "sel"), (f"ps{bB}",),
               tag=f"Bbc{c}_{d}")
            MM(ps[bC][:, 0:TCM], t_sel[0:33, (17 + d) * 128:(18 + d) * 128],
               t_projb[0:33, :], True, True, ("projb", "sel"), (f"ps{bC}",),
               tag=f"Cbc{c}_{d}")
            dp = d % 2
            P.op("scalar", lambda eng, bb=bB, dp=dp: eng.copy(
                     bf16v(o_Bs[dp], 128), ps[bb][:, 0:TCM]),
                 reads=(f"ps{bB}",), writes=(f"Bsb{dp}",), tag=f"Bev{c}_{d}")
            P.op("vector", lambda eng, dp=dp: eng.tensor_tensor(
                     bf16v(o_bxa, 2048).rearrange("p (r t) -> p r t", r=IC),
                     bf16v(o_hb, 2048).rearrange("p (r t) -> p r t", r=IC),
                     bf16v(o_Bs[dp], 128).unsqueeze(1).broadcast_to([128, IC, TCM]),
                     OP.mult),
                 reads=tuple(f"hb{j}" for j in range(IC)) + (f"Bsb{dp}",),
                 writes=("bxall",), tag=f"bxb{c}_{d}")
            P.op("scalar", lambda eng, bc=bC, dp=dp: eng.copy(
                     bf16v(o_Cs[dp], 128), ps[bc][:, 0:TCM]),
                 reads=(f"ps{bC}",), writes=(f"Csb{dp}",), tag=f"Cev{c}_{d}")
            for j in range(IC):
                it = d * IC + j
                par = it % 2
                dAa = f32v(o_sc + par * TCM, TCM)
                bxa = bf16v(o_bxa, 2048)[:, j * TCM:(j + 1) * TCM]
                sa = bf16v(o_sall, 2048)[:, j * TCM:(j + 1) * TCM]
                rdA, rbx = (f"dA{par}", "bxall")
                rs_ = f"s16_{j}"
                col = j * 16 + d
                P.op("scalar", lambda eng, a=dAa, j=j, col=col: eng.activation(
                         a, bf16v(o_dt, 2048)[:, j * TCM:(j + 1) * TCM], AF.Exp,
                         scale=t_Als[:, col:col + 1]),
                     reads=(f"dt{j}", "Als"), writes=(rdA,), tag=f"dA{c}_{d}_{j}")
                P.op("vector", lambda eng, a=sa, d0=dAa, d1=bxa, col=col:
                         eng.tensor_tensor_scan(
                             a, d0, d1, t_states[:, col:col + 1], OP.mult, OP.add),
                     reads=(rdA, rbx, f"st{col}"), writes=(rs_,), tag=f"scan{c}_{d}_{j}")

            s_regs = tuple(f"s16_{j}" for j in range(IC))
            st_regs = tuple(f"st{j * 16 + d}" for j in range(IC))
            P.op("vector", lambda eng, d=d: eng.tensor_copy(
                     t_states[:].rearrange("p (j k) -> p j k", k=16)[:, :, d:d + 1],
                     bf16v(o_sall, 2048).rearrange("p (j t) -> p j t", t=TCM)[:, :, TCM - 1:TCM]),
                 reads=s_regs, writes=st_regs, tag=f"stc{c}_{d}")
            sall3 = bf16v(o_sall, 2048).rearrange("p (r t) -> p r t", r=IC)
            y3 = f32v(o_y, 4096).rearrange("p (r t) -> p r t", r=IC)
            if d == 0:
                P.op("vector", lambda eng, dp=d % 2: eng.tensor_tensor(
                         y3, sall3,
                         bf16v(o_Cs[dp], 128).unsqueeze(1).broadcast_to([128, IC, TCM]),
                         OP.mult),
                     reads=s_regs + (f"Csb{d % 2}",), writes=("yall",),
                     tag=f"ro{c}_{d}")
            else:
                P.op("vector", lambda eng, dp=d % 2: eng.tensor_tensor(
                         sall3, sall3,
                         bf16v(o_Cs[dp], 128).unsqueeze(1).broadcast_to([128, IC, TCM]),
                         OP.mult),
                     reads=s_regs + (f"Csb{d % 2}",), writes=s_regs,
                     tag=f"ro{c}_{d}")
                P.op("vector", lambda eng: eng.tensor_tensor(
                         f32v(o_y, 4096), f32v(o_y, 4096), bf16v(o_sall, 2048), OP.add),
                     reads=s_regs + ("yall",), writes=("yall",), tag=f"ya{c}_{d}")

        for j in range(IC):
            ya = f32v(o_y + j * TCM, TCM)
            hbj = bf16v(o_hb, 2048)[:, j * TCM:(j + 1) * TCM]
            P.op("vector", lambda eng, a=ya, b=hbj, j=j: eng.scalar_tensor_tensor(
                     a, b, t_Dv[:, j:j + 1], a, OP.mult, OP.add),
                 reads=(f"hb{j}", "Dv", "yall"), writes=("yall",), tag=f"yD{c}_{j}")
        P.op("scalar", lambda eng: eng.activation(
                 bf16v(o_yb, 2048), f32v(o_y, 4096), AF.Silu),
             reads=("yall",), writes=tuple(f"yb{j}" for j in range(IC)),
             tag=f"ysl{c}")

        if debug and c == 0:
            tap("yfin", f32v(o_y, 4096), ("yall",), (128, 4096))
            tap("yb0", bf16v(o_yb, 2048), tuple(f"yb{j}" for j in range(IC)), (128, 4096))
        for h in range(HC):
            bank = 4 + (h % 2)
            for k in range(IC):
                lhs = bf16v(o_wo, 8192)[:, k * 1024 + h * 128: k * 1024 + (h + 1) * 128]
                rhs = bf16v(o_yb, 2048)[:, k * TCM:(k + 1) * TCM]
                MM(ps[bank][:, 0:TCM], lhs, rhs, k == 0, k == IC - 1,
                   ("wo", f"yb{k}"), (f"ps{bank}",), tag=f"op{c}_{h}")
            moa = f32v(o_mo + h * TCM, TCM)
            lna = f32v(o_lnx + h * TCM, TCM)
            P.op("vector", lambda eng, a=moa, b=lna, bk=bank: eng.tensor_tensor(
                     a, ps[bk][:, 0:TCM], b, OP.add),
                 reads=(f"ps{bank}", "lnx"), writes=("mo",), tag=f"ope{c}_{h}")

        if debug and c == 0:
            tap("mo0", f32v(o_mo, 2048), ("mo",), (128, 2048))
        msl = [(f32v(o_mo + h * TCM, TCM), "mo") for h in range(HC)]
        ln_stats(msl, lambda k: (f32v(o_sq + k * TCM, TCM), "sq"),
                 ps_sum=(ps[0], "ps0"), ps_sq=(ps[1], "ps1"),
                 n_feat=D, tc=TCM, regs_src=None, inst=f"lpc{c}")
        ln_rows((ps[0], "ps0"), (ps[1], "ps1"), D, TCM, f"lpc{c}",
                psr=(ps[2], "ps2"), psm=(ps[3], "ps3"))
        for h in range(HC):
            moa = f32v(o_mo + h * TCM, TCM)
            xa = f32v(o_xc + h * TCM, TCM)
            x1a = f32v(o_lnx + h * TCM, TCM)
            P.op("vector", lambda eng, a=x1a, b=moa: eng.tensor_tensor(
                     a, b, ps[2][:, 0:TCM], OP.mult),
                 reads=("mo", "ps2"), writes=("lnx",), tag=f"lpa{c}")
            P.op("vector", lambda eng, a=x1a: eng.tensor_tensor(
                     a, a, ps[3][:, 0:TCM], OP.subtract),
                 reads=("lnx", "ps3"), writes=("lnx",), tag=f"lpb{c}")
            P.op("vector", lambda eng, a=x1a, b=xa: eng.tensor_tensor(
                     a, a, b, OP.add),
                 reads=("lnx", "xc"), writes=("lnx",), tag=f"lpr{c}")
        if debug and c == 0:
            tap("x1c0", f32v(o_lnx, 2048), ("lnx",), (128, 2048))
        xdst = x1h.rearrange("(h p) t -> p h t", p=128)[:, :, cw0:cw0 + TCM]
        P.dma("sync", lambda eng, dp=xdst: eng.dma_start(out=dp, in_=f32v(o_lnx, 2048).rearrange("p (h t) -> p h t", h=8)),
              reads=("lnx",), writes=(f"x1h{c}",), tag=f"stx1{c}")

    P.fence()

    for st in range(NST if do_ffn else 0):
        s0 = st * SUP
        P.op("vector", lambda eng: eng.memset(f32v(o_fac, 4096), 0.0),
             writes=tuple(f"fac{h}" for h in range(4)), tag="fz")
        P.op("vector", lambda eng: eng.memset(f32v(o_fac2, 4096), 0.0),
             writes=tuple(f"fac{h}" for h in range(4, 8)), tag="fz2")

        for ts in range(NTS):
            tc0 = ts * FT
            x1src = x1h.rearrange("(h p) t -> p h t", p=128)[:, :, s0 + tc0:s0 + tc0 + FT]
            P.dma("sync", lambda eng, sp=x1src: eng.dma_start(
                      out=f32v(o_x1t, 4096).rearrange("p (h t) -> p h t", h=8), in_=sp),
                  reads=tuple(f"x1h{c}" for c in range(NCH)),
                  writes=("x1t",), tag=f"ldx1_{st}_{ts}")
            xsl = [(f32v(o_x1t + h * FT, FT), "x1t") for h in range(HC)]
            ln_stats(xsl, lambda k: (f32v(o_scr + k * FT, FT), "scr"),
                     ps_sum=(ps[0], "ps0"), ps_sq=(ps[1], "ps1"),
                     n_feat=D, tc=FT, regs_src=None, inst=f"l1_{st}_{ts}")
            ln_rows((ps[0], "ps0"), (ps[1], "ps1"), D, FT, f"l1_{st}_{ts}",
                    psr=(ps[2], "ps2"), psm=(ps[3], "ps3"))
            for h in range(HC):
                xa = f32v(o_x1t + h * FT, FT)
                fa = f32v(o_scr + h * FT, FT)
                la = bf16v(o_ln1, 4096)[:, h * SUP + tc0: h * SUP + tc0 + FT]
                P.op("vector", lambda eng, a=fa, b=xa: eng.tensor_tensor(
                         a, b, ps[2][:, 0:FT], OP.mult),
                     reads=("x1t", "ps2"), writes=("scr",), tag=f"l1a{st}{ts}")
                P.op("vector", lambda eng, a=fa: eng.tensor_tensor(
                         a, a, ps[3][:, 0:FT], OP.subtract),
                     reads=("scr", "ps3"), writes=("scr",), tag=f"l1b{st}{ts}")
                P.op("vector", lambda eng, a=la, b=fa: eng.tensor_copy(a, b),
                     reads=("scr",), writes=(f"ln1_{h}",), tag=f"l1c{st}{ts}")
            for k in range(HC):
                fa = f32v(o_scr + k * FT, FT)
                MM(ps[4][0:8, 0:FT], t_gw[:, k * 8:(k + 1) * 8], fa,
                   k == 0, k == HC - 1, ("scr", "gw"), ("ps4",), tag=f"gl{st}{ts}")
            P.op("scalar", lambda eng: eng.copy(
                     t_lg[0:8, 0:FT], ps[4][0:8, 0:FT]),
                 reads=("ps4",), writes=("lg",), tag=f"gle{st}{ts}")
            W4 = FT // 128
            for w in range(W4):
                P.op("tensor", lambda eng, w=w: eng.transpose(
                         ps[5][:, w * 8:(w + 1) * 8],
                         t_lg[0:8, w * 128:(w + 1) * 128], t_id[0:8, 0:8]),
                     reads=("lg", "ident"), writes=("ps5",), tag=f"rt1{st}{ts}{w}")
            for w in range(W4):
                P.op("vector", lambda eng, w=w: eng.reduce_max(
                         t_rsc[:, 32 + w:33 + w], ps[5][:, w * 8:(w + 1) * 8], AX.X),
                     reads=("ps5",), writes=("rtop",), tag=f"rt3{st}{ts}{w}")
            P.op("vector", lambda eng: eng.tensor_copy(
                     t_spc[0:1, 0:256], t_spc[0:1, 256:512]),
                 reads=("rtop",), writes=("rtop", "spc"), tag=f"rt3b{st}{ts}")
            P.op("scalar", lambda eng: eng.activation(
                     t_rsc[:, 36:40], t_rsc[:, 32:36], AF.Sigmoid),
                 reads=("rtop",), writes=("rsco",), tag=f"rt4{st}{ts}")
            P.op("scalar", lambda eng: eng.copy(
                     t_spc[0:1, 256:512], t_spc[0:1, 0:256]),
                 reads=("rsco",), writes=("rsco", "spc"), tag=f"rt4b{st}{ts}")
            for w in range(W4):
                P.op("vector", lambda eng, w=w: eng.tensor_scalar(
                         t_rsc[:, 40 + 9 * w:48 + 9 * w], ps[5][:, w * 8:(w + 1) * 8],
                         t_rsc[:, 32 + w:33 + w], None, OP.is_ge),
                     reads=("ps5", "rtop"), writes=("req",), tag=f"rt5{st}{ts}{w}")
            for w in range(W4):
                P.op("vector", lambda eng, w=w: eng.tensor_scalar(
                         t_rsc[:, 40 + 9 * w:48 + 9 * w], t_rsc[:, 40 + 9 * w:48 + 9 * w],
                         t_rsc[:, 36 + w:37 + w], None, OP.mult),
                     reads=("req", "rsco"), writes=("req",), tag=f"rt6{st}{ts}{w}")
            for w in range(W4):
                P.op("vector", lambda eng, w=w: eng.tensor_copy(
                         t_rsc[:, 48 + 9 * w:49 + 9 * w], t_rsc[:, 36 + w:37 + w]),
                     reads=("rsco",), writes=("req",), tag=f"rt7{st}{ts}{w}")
            P.op("vector", lambda eng: eng.tensor_copy(
                     t_spc[0:1, 256:512], t_spc[0:1, 0:256]),
                 reads=("req",), writes=("req", "spc"), tag=f"rt7b{st}{ts}")
            for w in range(W4):
                P.op("tensor", lambda eng, w=w: eng.transpose(
                         ps[6][0:9, w * 128:(w + 1) * 128],
                         t_rsc[:, 40 + 9 * w:49 + 9 * w], t_id[:, 0:128]),
                     reads=("req", "ident"), writes=("ps6",), tag=f"rt8{st}{ts}{w}")
            P.op("vector", lambda eng, tc0=tc0: eng.tensor_copy(
                     t_sm[0:9, tc0:tc0 + FT], ps[6][0:9, 0:FT]),
                 reads=("ps6",), writes=("smr",), tag=f"rt9{st}{ts}")
            if debug and st == 0 and ts == 0:
                tap("rsc0", t_rsc[:, 0:96], ("rtop", "rsco", "req"), (128, 96))
                tap("lg0", t_lg[0:8, 0:FT], ("lg",), (8, FT))

        for ei in range(9):
            routed = ei < 8
            if routed:
                for ts in range(NTS):
                    tc0 = ts * FT
                    MM(ps[7][:, 0:FT], t_sel[0:9, ei * 128:(ei + 1) * 128],
                       t_sm[0:9, tc0:tc0 + FT],
                       True, True, ("smr", "sel"), ("ps7",), tag=f"smb{st}_{ei}")
                    for h in range(HC):
                        la = bf16v(o_ln1, 4096)[:, h * SUP + tc0: h * SUP + tc0 + FT]
                        xa = bf16v(o_xs, 4096)[:, h * SUP + tc0: h * SUP + tc0 + FT]
                        P.op("vector", lambda eng, a=xa, b=la: eng.tensor_tensor(
                                 a, b, ps[7][:, 0:FT], OP.mult),
                             reads=(f"ln1_{h}", "ps7"), writes=(f"xs{h}",),
                             tag=f"xs{st}_{ei}_{h}")
            for ts in range(NTS):
                tc0 = ts * FT
                for q in range(4):
                    sidx = q % 2
                    i0 = q * 512
                    if routed:
                        gsrc, rg = rtg_b[ei, :, i0:i0 + 512], (f"wg{sidx}", f"wcg{ei}")
                        usrc, ru = rtu_b[ei, :, i0:i0 + 512], (f"wu{sidx}", f"wcu{ei}")
                        dsrc, rd2 = rtd_b[ei, i0:i0 + 512, :], (f"wd{sidx}", f"wcd{ei}")
                    else:
                        gsrc, rg = shg_b[:, i0:i0 + 512], (f"wg{sidx}", "wcg8")
                        usrc, ru = shu_b[:, i0:i0 + 512], (f"wu{sidx}", "wcu8")
                        dsrc, rd2 = shd_b[i0:i0 + 512, :], (f"wd{sidx}", "wcd8")
                    P.dma("sync", lambda eng, d_=bf16v(o_wg[sidx], 2048).rearrange("p (c i) -> p c i", c=8),
                          s_=gsrc.rearrange("(c p) i -> p c i", p=128): eng.dma_start(out=d_, in_=s_),
                          reads=rg[1:], writes=rg[:1], tag=f"lwg{st}_{ei}_{ts}_{q}")
                    P.dma("sync", lambda eng, d_=bf16v(o_wu[sidx], 2048).rearrange("p (c i) -> p c i", c=8),
                          s_=usrc.rearrange("(c p) i -> p c i", p=128): eng.dma_start(out=d_, in_=s_),
                          reads=ru[1:], writes=ru[:1], tag=f"lwu{st}_{ei}_{ts}_{q}")
                    P.dma("sync", lambda eng, d_=bf16v(o_wd[sidx], 2048).rearrange("p (c h) -> p c h", c=4),
                          s_=dsrc.rearrange("(c p) h -> p c h", p=128): eng.dma_start(out=d_, in_=s_),
                          reads=rd2[1:], writes=rd2[:1], tag=f"lwd{st}_{ei}_{ts}_{q}")
                    for il in range(4):
                        bg = (il % 2) * 2
                        bu = bg + 1
                        for k in range(HC):
                            lhs = bf16v(o_wg[sidx], 2048)[:, k * 512 + il * 128:
                                                          k * 512 + (il + 1) * 128]
                            if routed:
                                rhs = bf16v(o_xs, 4096)[:, k * SUP + tc0:
                                                        k * SUP + tc0 + FT]
                                rr = f"xs{k}"
                            else:
                                rhs = bf16v(o_ln1, 4096)[:, k * SUP + tc0:
                                                         k * SUP + tc0 + FT]
                                rr = f"ln1_{k}"
                            MM(ps[bg][:, 0:FT], lhs, rhs, k == 0, k == HC - 1,
                               (f"wg{sidx}", rr), (f"ps{bg}",), tag=f"g{ei}{ts}{q}{il}")
                        for k in range(HC):
                            lhs = bf16v(o_wu[sidx], 2048)[:, k * 512 + il * 128:
                                                          k * 512 + (il + 1) * 128]
                            if routed:
                                rhs = bf16v(o_xs, 4096)[:, k * SUP + tc0:
                                                        k * SUP + tc0 + FT]
                                rr = f"xs{k}"
                            else:
                                rhs = bf16v(o_ln1, 4096)[:, k * SUP + tc0:
                                                         k * SUP + tc0 + FT]
                                rr = f"ln1_{k}"
                            MM(ps[bu][:, 0:FT], lhs, rhs, k == 0, k == HC - 1,
                               (f"wu{sidx}", rr), (f"ps{bu}",), tag=f"u{ei}{ts}{q}{il}")
                        tpar = il % 2
                        tsa = t_tss[:, tpar * 512:tpar * 512 + FT]
                        P.op("scalar", lambda eng, a=tsa, bg=bg: eng.activation(
                                 a, ps[bg][:, 0:FT], AF.Silu),
                             reads=(f"ps{bg}",), writes=(f"tss{tpar}",),
                             tag=f"sg{ei}{ts}{q}{il}")
                        aa = bf16v(o_act[sidx], 1024)[:, il * FT:(il + 1) * FT]
                        P.op("vector", lambda eng, a=aa, b=tsa, bu=bu:
                                 eng.tensor_tensor(a, b, ps[bu][:, 0:FT], OP.mult),
                             reads=(f"tss{tpar}", f"ps{bu}"), writes=(f"act{sidx}_{il}",),
                             tag=f"am{ei}{ts}{q}{il}")
                    for hg in range(2):
                        for hh in range(4):
                            h = hg * 4 + hh
                            bank = 4 + hh
                            for il in range(4):
                                lhs = bf16v(o_wd[sidx], 2048)[:, il * 1024 + h * 128:
                                                              il * 1024 + (h + 1) * 128]
                                rhs = bf16v(o_act[sidx], 1024)[:, il * FT:(il + 1) * FT]
                                MM(ps[bank][:, 0:FT], lhs, rhs, il == 0, il == 3,
                                   (f"wd{sidx}", f"act{sidx}_{il}"), (f"ps{bank}",),
                                   tag=f"dn{ei}{ts}{q}{h}")
                        for hh in range(4):
                            h = hg * 4 + hh
                            bank = 4 + hh
                            fa = fac_slice(h, tc0, FT)
                            P.op("vector", lambda eng, a=fa, bk=bank:
                                     eng.tensor_tensor(a, a, ps[bk][:, 0:FT], OP.add),
                                 reads=(f"ps{bank}", f"fac{h}"), writes=(f"fac{h}",),
                                 tag=f"fa{ei}{ts}{q}{h}")

        for ts in range(NTS):
            tc0 = ts * FT
            x1src = x1h.rearrange("(h p) t -> p h t", p=128)[:, :, s0 + tc0:s0 + tc0 + FT]
            P.dma("sync", lambda eng, sp=x1src: eng.dma_start(
                      out=f32v(o_x1t, 4096).rearrange("p (h t) -> p h t", h=8), in_=sp),
                  reads=tuple(f"x1h{c}" for c in range(NCH)),
                  writes=("x1t",), tag=f"ldx1b_{st}_{ts}")
            fsl = [(fac_slice(h, tc0, FT), f"fac{h}") for h in range(HC)]
            ln_stats(fsl, lambda k: (f32v(o_scr + k * FT, FT), "scr"),
                     ps_sum=(ps[0], "ps0"), ps_sq=(ps[1], "ps1"),
                     n_feat=D, tc=FT, regs_src=None, inst=f"l2_{st}_{ts}")
            ln_rows((ps[0], "ps0"), (ps[1], "ps1"), D, FT, f"l2_{st}_{ts}",
                    psr=(ps[2], "ps2"), psm=(ps[3], "ps3"))
            for h in range(HC):
                fa = fac_slice(h, tc0, FT)
                xa = f32v(o_x1t + h * FT, FT)
                oa = f32v(o_out + h * FT, FT)
                P.op("vector", lambda eng, a=oa, b=fa: eng.tensor_tensor(
                         a, b, ps[2][:, 0:FT], OP.mult),
                     reads=(f"fac{h}", "ps2"), writes=("outb",), tag=f"l2a{st}{ts}")
                P.op("vector", lambda eng, a=oa: eng.tensor_tensor(
                         a, a, ps[3][:, 0:FT], OP.subtract),
                     reads=("outb", "ps3"), writes=("outb",), tag=f"l2b{st}{ts}")
                P.op("vector", lambda eng, a=oa, b=xa: eng.tensor_tensor(
                         a, a, b, OP.add),
                     reads=("outb", "x1t"), writes=("outb",), tag=f"l2r{st}{ts}")
            odst = outT.rearrange("(h p) t -> p h t", p=128)[:, :, s0 + tc0:s0 + tc0 + FT]
            P.dma("sync", lambda eng, dp=odst: eng.dma_start(
                      out=dp, in_=f32v(o_out, 4096).rearrange("p (h t) -> p h t", h=8)),
                  reads=("outb",), writes=(f"outh{st}{ts}",), tag=f"sto{st}{ts}")

    P.analyze()
    bodyf = P.emit(nc, sems)

    with nc.Block() as block:
        @block.sync
        def _(eng):
            bodyf(eng, "sync")
            eng.wait_ge(sems["qsync"], 16 * P.final_qcnt["qsync"])
            eng.wait_ge(sems["qgps"], 16 * P.final_qcnt["qgps"])

        @block.gpsimd
        def _(eng):
            bodyf(eng, "gpsimd")

        @block.vector
        def _(eng):
            bodyf(eng, "vector")

        @block.scalar
        def _(eng):
            bodyf(eng, "scalar")

        @block.tensor
        def _(eng):
            bodyf(eng, "tensor")

    ctx.close()
    return nc


_NC = None
DEBUG = False


def _host_inputs(inputs):
    g = {}
    for k, v in inputs.items():
        a = np.asarray(v)
        if a.dtype != np.float32:
            a = a.astype(np.float32)
        g[k] = a
    x = g["L0_attn_x"]
    base = {}
    base["wiT"] = np.ascontiguousarray(g["L0_attn_in_proj_w"].T)
    base["woT"] = np.ascontiguousarray(g["L0_attn_out_proj_w"].T)
    base["xpT"] = np.ascontiguousarray(g["L0_attn_x_proj_w"].T)
    base["Als"] = np.ascontiguousarray(
        g["L0_attn_A_log"].reshape(16, 128, 16).transpose(1, 0, 2).reshape(128, 256))
    base["wdt"] = np.ascontiguousarray(g["L0_attn_dt_proj_w"].reshape(16, 128).T)
    base["bdt"] = np.ascontiguousarray(g["L0_attn_dt_proj_b"].reshape(16, 128).T)
    base["cw"] = np.ascontiguousarray(
        g["L0_attn_conv_w"].reshape(16, 128, 4).transpose(1, 0, 2).reshape(128, 64))
    base["Dv"] = np.ascontiguousarray(g["L0_attn_D"].reshape(16, 128).T)
    base["nw"] = np.ascontiguousarray(g["L0_attn_norm_w"].reshape(8, 128).T)
    base["nb"] = np.ascontiguousarray(g["L0_attn_norm_b"].reshape(8, 128).T)
    base["gw"] = np.ascontiguousarray(g["L0_ffn_gate_w"])
    base["ones34"] = np.ones((34, 128), np.float32)
    sel = np.zeros((33, 33, 128), np.float32)
    for jj in range(33):
        sel[jj, jj, :] = 1.0
    base["selmat"] = np.ascontiguousarray(sel.transpose(1, 0, 2).reshape(33, 33 * 128))
    base["ident"] = np.eye(128, dtype=np.float32)
    base["onescol"] = np.ones((128, 1), np.float32)
    base["shg"] = np.ascontiguousarray(g["L0_ffn_sh_gate"])
    base["shu"] = np.ascontiguousarray(g["L0_ffn_sh_up"])
    base["shd"] = np.ascontiguousarray(g["L0_ffn_sh_down"])
    base["rtg"] = np.ascontiguousarray(g["L0_ffn_rt_gate"])
    base["rtu"] = np.ascontiguousarray(g["L0_ffn_rt_up"])
    base["rtd"] = np.ascontiguousarray(g["L0_ffn_rt_down"])
    in_maps = []
    for c in range(8):
        m = dict(base)
        m["xT"] = np.ascontiguousarray(x[c].T)
        in_maps.append(m)
    return in_maps


def kernel(**inputs):
    global _NC
    if _NC is None:
        _NC = build_nc()
    in_maps = _host_inputs(inputs)
    res = run_bass_kernel_spmd(_NC, in_maps, core_ids=list(range(8)))
    out = np.stack([np.asarray(res.results[c]["outT"]).T for c in range(8)], axis=0)
    return np.ascontiguousarray(out.astype(np.float32))
